# version 13
# baseline (speedup 1.0000x reference)
import numpy as np
import ml_dtypes
import concourse.bass as bass
import concourse.mybir as mybir
from concourse.bass_utils import run_bass_kernel_spmd

F32 = mybir.dt.float32
BF16 = mybir.dt.bfloat16
I32 = mybir.dt.int32
AX = mybir.AxisListType.X
AF = mybir.ActivationFunctionType
ALU = mybir.AluOpType

B, S, H = 2, 8192, 1024
NCORES = 8
TOK = 2048
EXT = 2176
NTILE = 17
SCALE = 0.125
FH, HC = 3072, 24
QS = 1.0 / 127.0
QS2 = QS * QS
NA = 4
NQB = 16
EPS = 1e-5
NEG = -1e30

ENGINES = ("sync", "gpsimd", "tensor", "vector", "scalar")
DMA_ENG = {"sync": True, "gpsimd": True, "tensor": False, "vector": False,
           "scalar": False}


class Prog:

    def __init__(self):
        self.ops = {e: [] for e in ENGINES}
        self.nkey = {}
        self.last_of = {}

    def emit(self, eng, fn, waits=(), stream=None):
        key = stream if stream is not None else eng
        idx = self.nkey.get(key, 0)
        self.nkey[key] = idx + 1
        tok = (key, idx)
        self.ops[eng].append((fn, [w for w in waits if w is not None], key))
        self.last_of[key] = tok
        return tok

    def last(self, eng):
        return self.last_of.get(eng)

    def barrier_tokens(self):
        return list(self.last_of.values())


def build_nc(stage="full"):
    nc = bass.Bass("TRN2", target_bir_lowering=False)
    p = Prog()

    x_tok_h = nc.dram_tensor("x_tok", [EXT, H], F32, kind="ExternalInput").ap()
    wq_h = nc.dram_tensor("wq", [H, H], F32, kind="ExternalInput").ap()
    wk_h = nc.dram_tensor("wk", [H, H], F32, kind="ExternalInput").ap()
    wv_h = nc.dram_tensor("wv", [H, H], F32, kind="ExternalInput").ap()
    wo_h = nc.dram_tensor("wo", [H, H], F32, kind="ExternalInput").ap()
    bq_h = nc.dram_tensor("bq", [128, 8], F32, kind="ExternalInput").ap()
    bk_h = nc.dram_tensor("bk", [128, 8], F32, kind="ExternalInput").ap()
    bv_h = nc.dram_tensor("bv", [128, 8], F32, kind="ExternalInput").ap()
    bo_h = nc.dram_tensor("bo_row", [1, H], F32, kind="ExternalInput").ap()
    wg_hh = nc.dram_tensor("wg_b", [FH, H], I32, kind="ExternalInput").ap()
    wu_hh = nc.dram_tensor("wu_b", [FH, H], I32, kind="ExternalInput").ap()
    wd_hh = nc.dram_tensor("wd_b", [FH, H], I32, kind="ExternalInput").ap()
    mskf_h = nc.dram_tensor("mask_first", [128, 268], BF16, kind="ExternalInput").ap()
    mskm_h = nc.dram_tensor("mask_mid", [128, 268], BF16, kind="ExternalInput").ap()
    mskl_h = nc.dram_tensor("mask_last", [128, 268], BF16, kind="ExternalInput").ap()
    idn_h = nc.dram_tensor("ident", [128, 128], BF16, kind="ExternalInput").ap()
    ones_h = nc.dram_tensor("ones_row", [1, 128], BF16, kind="ExternalInput").ap()
    eps_h = nc.dram_tensor("eps_col", [128, 1], F32, kind="ExternalInput").ap()
    y_h = nc.dram_tensor("y", [TOK, H], F32, kind="ExternalOutput").ap()

    SBUF_BASE = 16640
    SBUF_LIMIT = 229376
    _off = [SBUF_BASE]

    def at(name, shape, dtype, off=None):
        nb = shape[-1] * mybir.dt.size(dtype)
        if off is None:
            off = _off[0]
            _off[0] = off + ((nb + 255) // 256) * 256
            assert _off[0] <= SBUF_LIMIT, (name, _off[0])
        return nc.alloc_sbuf_tensor_at(name, list(shape), dtype, offset=off)

    mskf = at("mskf", [128, 268], BF16)
    mskm = at("mskm", [128, 268], BF16)
    mskl = at("mskl", [128, 268], BF16)
    bq_sb = at("bq_sb", [128, 8], F32)
    bk_sb = at("bk_sb", [128, 8], F32)
    bv_sb = at("bv_sb", [128, 8], BF16)
    bqs_sb = at("bqs_sb", [128, 8], F32)
    bo_sb = at("bo_sb", [1, H], F32)
    beff_sb = at("beff_sb", [1, H], BF16)
    idn_sb = at("idn_sb", [128, 128], BF16)
    ones_sb = at("ones_sb", [1, 128], BF16)
    eps_sb = at("eps_sb", [128, 1], F32)
    st = {k: at("st_" + k, [128, NTILE], F32)
          for k in ("s1", "s2", "ms", "m2", "va", "ln", "a", "b")}
    acc = at("acc", [128, NQB * H], BF16)
    y_sb = [at(f"y_sb{i}", [128, H], F32) for i in range(2)]

    xn0 = _off[0]
    xn = [at(f"xn{ic}", [128, EXT], BF16) for ic in range(8)]
    h0 = _off[0]
    h_sb = [at(f"h{c}", [128, TOK], BF16) for c in range(HC)]
    top0 = _off[0]
    wgst = [at(f"wgst{i}", [128, H], BF16) for i in range(2)]
    wust = [at(f"wust{i}", [128, H], BF16) for i in range(2)]
    htmp = at("htmp", [128, TOK], BF16)
    assert _off[0] <= SBUF_LIMIT

    wdT = []
    for c in range(HC):
        off = xn0 + 2048 * c if c < 17 else top0 + 2048 * (c - 17)
        wdT.append(at(f"wdT{c}", [128, H], BF16, off=off))

    a_off = [h0]

    def atA(name, shape, dtype):
        nb = shape[-1] * mybir.dt.size(dtype)
        off = a_off[0]
        a_off[0] = off + ((nb + 255) // 256) * 256
        assert a_off[0] <= SBUF_LIMIT, (name, a_off[0])
        return nc.alloc_sbuf_tensor_at(name, list(shape), dtype, offset=off)

    wq_sb = [atA(f"wqb{i}", [128, H], BF16) for i in range(8)]
    wk_sb = [atA(f"wkb{i}", [128, H], BF16) for i in range(8)]
    wv_sb = [atA(f"wvb{i}", [128, H], BF16) for i in range(8)]
    wo_sb = [atA(f"wob{i}", [128, H], BF16) for i in range(8)]
    ln0 = a_off[0]
    q_fm = [atA(f"qfm{i}", [128, 512], BF16) for i in range(8)]
    k_fm = [atA(f"kfm{i}", [128, 528], BF16) for i in range(8)]
    v_tok = atA("v_tok", [128, 5 * H], BF16)
    o_fm = [atA(f"ofm{i}", [128, 512], BF16) for i in range(8)]
    s_m = [atA(f"s_m{i}", [128, 268], F32) for i in range(2)]
    p_sb = [atA(f"p_sb{i}", [128, 268], BF16) for i in range(2)]
    p_n = [atA(f"p_n{i}", [128, 268], BF16) for i in range(4)]
    nm_sb = [atA(f"nm{i}", [128, 2], F32) for i in range(2)]
    z_sb = [atA(f"z{i}", [128, 2], F32) for i in range(2)]
    rz_sb = [atA(f"rz{i}", [128, 2], F32) for i in range(2)]
    at1 = [atA(f"at1_{i}", [128, 512], BF16) for i in range(2)]
    at2 = [atA(f"at2_{i}", [6, 512], BF16) for i in range(2)]
    x_res = [atA(f"xres{i}", [128, H], F32) for i in range(2)]
    lnoff = [ln0]

    def atLN(name, shape, dtype):
        nb = shape[-1] * mybir.dt.size(dtype)
        off = lnoff[0]
        lnoff[0] = off + ((nb + 255) // 256) * 256
        return nc.alloc_sbuf_tensor_at(name, list(shape), dtype, offset=off)

    xst = [atLN(f"xst{i}", [128, H], F32) for i in range(2)]
    xnst = [atLN(f"xnst{i}", [128, H], BF16) for i in range(2)]
    sq_scr = atLN("sq_scr", [128, H], BF16)

    pb = [nc.alloc_psum_tensor(f"pb{i}", [128, 1024], BF16) for i in range(8)]
    pbf = [t.ap().bitcast(F32) for t in pb]

    E = p.emit

    t_consts = []
    for dst, src in ((mskf, mskf_h), (mskm, mskm_h), (mskl, mskl_h),
                     (bq_sb, bq_h), (bk_sb, bk_h),
                     (bo_sb, bo_h), (idn_sb, idn_h), (ones_sb, ones_h),
                     (eps_sb, eps_h)):
        t_consts.append(E("sync", lambda e, d=dst, s=src: e.dma_start(d[:], s[:]),
                          stream="dC"))
    t_cAll = t_consts[-1]
    t_bv = E("gpsimd", lambda e: e.dma_start(bv_sb[:], bv_h[:]), stream="dBv")

    t_w = {}
    wlist = (() if stage == "ln0" else
             (("q", wq_sb, wq_h), ("k", wk_sb, wk_h),
              ("v", wv_sb, wv_h), ("o", wo_sb, wo_h)))
    for nm, bufs, src in wlist:
        t = None
        for ic in range(8):
            t = E("gpsimd",
                  lambda e, d=bufs[ic], s=src, i=ic:
                      e.dma_start(d[:], s[i * 128:(i + 1) * 128, :]),
                  stream="dW" + nm)
        t_w[nm] = t

    t_bqs = E("vector", lambda e: e.tensor_scalar_mul(bqs_sb[:], bq_sb[:], SCALE),
              waits=[t_cAll])

    t_beff = []
    bank_free = [None] * 8
    bank_done = [None] * 8
    for half in range(2 if stage != "ln0" else 0):
        tk = None
        for ic in range(8):
            tk = E("tensor",
                   lambda e, i=ic, t=pbf[half], hh=half:
                       e.matmul(t[0:1, :], bv_sb[:, i:i + 1],
                                wo_sb[i][:, hh * 512:(hh + 1) * 512],
                                start=(i == 0), stop=(i == 7),
                                skip_group_check=True),
                   waits=([t_w["o"], t_bv] if ic == 0 else []))
        bank_done[half] = tk
        tv = E("vector",
               lambda e, t=pbf[half], hh=half:
                   e.tensor_tensor(beff_sb[0:1, hh * 512:(hh + 1) * 512],
                                   t[0:1, :], bo_sb[0:1, hh * 512:(hh + 1) * 512],
                                   ALU.add),
               waits=[bank_done[half], t_cAll])
        bank_free[half] = tv
        t_beff.append(tv)

    inv_h = 1.0 / H
    xst_guard = [[], []]
    xnst_guard = [[], []]
    ln_bank_guard = [[], []]
    for i in range(NTILE):
        par = i % 2
        t_x = E("sync",
                lambda e, j=i, d=xst[par]:
                    e.dma_start(d[:], x_tok_h[j * 128:(j + 1) * 128, :]),
                waits=xst_guard[par], stream=f"dX{par}")
        t_s1 = E("vector",
                 lambda e, j=i, s=xst[par]:
                     e.reduce_sum(st["s1"][:, j:j + 1], s[:], axis=AX),
                 waits=[t_x])
        t_sq = E("scalar",
                 lambda e, j=i, s=xst[par]:
                     e.activation(sq_scr[:], s[:], AF.Square,
                                  accum_out=st["s2"][:, j:j + 1]),
                 waits=[t_x])
        t_ms = E("vector",
                 lambda e, j=i:
                     e.tensor_scalar_mul(st["ms"][:, j:j + 1],
                                         st["s1"][:, j:j + 1], inv_h),
                 waits=[t_s1])
        t_m2 = E("vector",
                 lambda e, j=i:
                     e.tensor_mul(st["m2"][:, j:j + 1], st["ms"][:, j:j + 1],
                                  st["ms"][:, j:j + 1]),
                 waits=[t_ms])
        t_va = E("vector",
                 lambda e, j=i:
                     e.scalar_tensor_tensor(st["va"][:, j:j + 1],
                                            st["s2"][:, j:j + 1], inv_h,
                                            st["m2"][:, j:j + 1],
                                            ALU.mult, ALU.subtract),
                 waits=[t_sq, t_m2])
        t_ln = E("scalar",
                 lambda e, j=i:
                     e.activation(st["ln"][:, j:j + 1], st["va"][:, j:j + 1],
                                  AF.Ln, bias=eps_sb[:, 0:1]),
                 waits=[t_va, t_cAll])
        t_a = E("scalar",
                lambda e, j=i:
                    e.activation(st["a"][:, j:j + 1], st["ln"][:, j:j + 1],
                                 AF.Exp, scale=-0.5),
                waits=[t_ln])
        t_b = E("vector",
                lambda e, j=i:
                    e.tensor_scalar(st["b"][:, j:j + 1], st["ms"][:, j:j + 1],
                                    st["a"][:, j:j + 1], -1.0,
                                    ALU.mult, ALU.mult),
                waits=[t_a])
        t_ap = E("scalar",
                 lambda e, j=i, s=xst[par], d=xnst[par]:
                     e.activation(d[:], s[:], AF.Identity,
                                  bias=st["b"][:, j:j + 1],
                                  scale=st["a"][:, j:j + 1]),
                 waits=[t_b] + xnst_guard[par])
        xst_guard[par] = [t_sq, t_ap]
        t_tr = None
        for ic in range(8):
            t_tr = E("tensor",
                     lambda e, j=i, c=ic, s=xnst[par], bk=pb[6 + par]:
                         e.transpose(bk[:, c * 128:(c + 1) * 128],
                                     s[:, c * 128:(c + 1) * 128], idn_sb[:]),
                     waits=([t_ap, t_cAll] + ln_bank_guard[par]
                            if ic == 0 else []))
        xnst_guard[par] = [t_tr]
        evs = []
        eng = "vector" if par == 0 else "scalar"
        for ic in range(8):
            fn = ((lambda e, j=i, c=ic, bk=pb[6 + par]:
                       e.tensor_copy(xn[c][:, j * 128:(j + 1) * 128],
                                     bk[:, c * 128:(c + 1) * 128]))
                  if eng == "vector" else
                  (lambda e, j=i, c=ic, bk=pb[6 + par]:
                       e.activation(xn[c][:, j * 128:(j + 1) * 128],
                                    bk[:, c * 128:(c + 1) * 128],
                                    AF.Identity)))
            evs.append(E(eng, fn, waits=[t_tr] if ic == 0 else []))
        ln_bank_guard[par] = [evs[7]]
    t_ln_all = p.barrier_tokens()

    gd = {"exps": {}, "pn": {}, "tp": {}, "xres": [[], []],
          "qkv": [], "ofm": [], "first_xres": True,
          "avm": {}, "atw": [None], "att": [None, None], "xr": {}}
    t_last_y = [None]

    def dump_and_finish(kind):
        tprev = None
        base = p.barrier_tokens()
        if kind == "xn":
            for ki, kk in enumerate(("s1", "s2", "ms", "m2", "va", "ln", "a", "b")):
                tprev = E("sync",
                          lambda e, k=ki, t=st[kk]:
                              e.dma_start(y_h[1024 + 128 * k:1152 + 128 * k, 0:NTILE],
                                          t[:]),
                          waits=list(base) + ([tprev] if tprev else []),
                          stream="dY0")
        n = 8 if kind == "xn" else NQB
        for i in range(n):
            w = list(base) + ([tprev] if tprev else [])
            if kind == "xn":
                tv = E("vector",
                       lambda e, c=i, d=y_sb[i % 2]:
                           e.tensor_copy(d[:], xn[c][:, 0:H]), waits=w)
            elif kind == "acc":
                tv = E("vector",
                       lambda e, t=i, d=y_sb[i % 2]:
                           e.tensor_copy(d[:], acc[:, t * H:(t + 1) * H]),
                       waits=w)
            else:
                tv = E("vector",
                       lambda e, c=i, d=y_sb[i % 2]:
                           e.tensor_copy(d[:], h_sb[c][:, 0:H]), waits=w)
            tprev = E("sync",
                      lambda e, t=i, s=y_sb[i % 2]:
                          e.dma_start(y_h[t * 128:(t + 1) * 128, :], s[:]),
                      waits=[tv], stream=f"dY{i % 2}")
        t_last_y[0] = tprev

    def emit_atile(a, upto="full"):
        ext0 = 512 * a
        ln_wait = list(t_ln_all) if a == 0 else []

        for jc in range(8):
            bi = jc % 2
            tk = None
            for ic in range(8):
                tk = E("tensor",
                       lambda e, i=ic, j=jc, t=pbf[bi], e0=ext0:
                           e.matmul(t[:], wq_sb[i][:, j * 128:(j + 1) * 128],
                                    xn[i][:, e0 + 3:e0 + 515],
                                    start=(i == 0), stop=(i == 7),
                                    skip_group_check=True),
                       waits=([bank_free[bi], t_w["q"], t_bqs] + ln_wait
                              if ic == 0 else []))
            bank_done[bi] = tk
            bank_free[bi] = E("scalar",
                              lambda e, j=jc, t=pbf[bi]:
                                  e.activation(q_fm[j][:], t[:], AF.Identity,
                                               bias=bqs_sb[:, j:j + 1],
                                               scale=SCALE),
                              waits=[bank_done[bi]] + list(gd["qkv"]))
        t_qev = [bank_free[0], bank_free[1]]
        for jc in range(8):
            for grp, (c0, cw) in enumerate(((0, 512), (512, 6))):
                bi = grp
                tk = None
                for ic in range(8):
                    tk = E("tensor",
                           lambda e, i=ic, j=jc, t=pbf[bi], e0=ext0,
                                  cc=c0, ww=cw:
                               e.matmul(t[:, 0:ww],
                                        wk_sb[i][:, j * 128:(j + 1) * 128],
                                        xn[i][:, e0 + cc:e0 + cc + ww],
                                        start=(i == 0), stop=(i == 7),
                                        skip_group_check=True),
                           waits=([bank_free[bi], t_w["k"], t_cAll]
                                  + ln_wait if ic == 0 else []))
                bank_done[bi] = tk
                bank_free[bi] = E("scalar",
                                  lambda e, j=jc, t=pbf[bi], cc=c0, ww=cw:
                                      e.activation(k_fm[j][:, cc:cc + ww],
                                                   t[:, 0:ww], AF.Identity,
                                                   bias=bk_sb[:, j:j + 1]),
                                  waits=[bank_done[bi]] + list(gd["qkv"]))
        t_kev = [bank_free[0], bank_free[1]]
        for tb in range(5):
            for half in range(2):
                bi = half
                tk = None
                for ic in range(8):
                    tk = E("tensor",
                           lambda e, i=ic, t=pbf[bi], m=4 * a + tb, hh=half:
                               e.matmul(t[:],
                                        xn[i][:, m * 128:m * 128 + 128],
                                        wv_sb[i][:, hh * 512:(hh + 1) * 512],
                                        start=(i == 0), stop=(i == 7),
                                        skip_group_check=True),
                           waits=([bank_free[bi], t_w["v"]] + ln_wait
                                  if ic == 0 else []))
                bank_done[bi] = tk
                bank_free[bi] = E("scalar",
                                  lambda e, t=pbf[bi], s=tb, hh=half:
                                      e.activation(
                                          v_tok[:, s * H + hh * 512:
                                                s * H + hh * 512 + 512],
                                          t[:], AF.Identity),
                                  waits=[bank_done[bi]] + list(gd["qkv"]))
        t_vev = [bank_free[0], bank_free[1]]
        t_qkv = t_qev + t_kev + t_vev
        if upto == "qkv":
            return

        def msel(jl):
            g = 4 * a + jl
            return mskf if g == 0 else (mskl if g == NQB - 1 else mskm)

        def sc_pair(pr):
            gp = 32 * a + pr
            jl, hp = pr // 8, pr % 8
            m = msel(pr // 8)
            toks = []
            for s in range(2):
                bi = (2 if s == 0 else 4) + gp % 2
                hd = hp * 2 + s
                jc, off = hd // 2, (hd % 2) * 64
                toks.append(E("tensor",
                              lambda e, t=pbf[bi], jj=jl, c=jc, o=off:
                                  e.matmul(t[:, 0:134],
                                           q_fm[c][o:o + 64,
                                                   jj * 128:jj * 128 + 128],
                                           k_fm[c][o:o + 64,
                                                   jj * 128:jj * 128 + 134],
                                           start=True, stop=False,
                                           skip_group_check=True),
                              waits=[bank_free[bi]] + (t_qkv if s == 0 else [])))
                toks.append(E("tensor",
                              lambda e, t=pbf[bi], mm=m, ss=s:
                                  e.matmul(t[:, 0:134], idn_sb[:],
                                           mm[:, ss * 134:ss * 134 + 134],
                                           start=False, stop=True,
                                           skip_group_check=True),
                              waits=[t_cAll] if gp < 10 else []))
                bank_done[bi] = toks[-1]
            gd[("scmm", gp)] = toks

        def sm_pair(pr):
            gp = 32 * a + pr
            par = gp % 2
            t_exps = []
            for s in range(2):
                sbi = (2 if s == 0 else 4) + gp % 2
                t_nm = E("vector",
                         lambda e, t=pbf[sbi], d=nm_sb[par], ss=s:
                             e.reduce_max(d[:, ss:ss + 1], t[:, 0:134],
                                          axis=AX, negate=True),
                         waits=[bank_done[sbi]] +
                               (gd["exps"].get(gp - 2, []) if s == 0 else []))
                t_exps.append(E("scalar",
                                lambda e, t=pbf[sbi], ss=s, d=p_sb[par],
                                       nn=nm_sb[par], zz=z_sb[par]:
                                    e.activation(d[:, ss * 134:ss * 134 + 134],
                                                 t[:, 0:134],
                                                 AF.Exp, bias=nn[:, ss:ss + 1],
                                                 accum_out=zz[:, ss:ss + 1]),
                                waits=[t_nm] +
                                      (gd["pn"].get(gp - 2, [])
                                       if s == 0 else [])))
                bank_free[sbi] = t_exps[-1]
            gd["exps"][gp] = t_exps
            t_rz = E("vector",
                     lambda e, zz=z_sb[par], rr=rz_sb[par]:
                         e.reciprocal(rr[:], zz[:]),
                     waits=list(t_exps))
            t_pn = []
            for s in range(2):
                t_pn.append(E("vector",
                              lambda e, ss=s, src=p_sb[par], d=p_n[gp % 4],
                                     rr=rz_sb[par]:
                                  e.tensor_scalar_mul(
                                      d[:, ss * 134:ss * 134 + 134],
                                      src[:, ss * 134:ss * 134 + 134],
                                      rr[:, ss:ss + 1]),
                              waits=([t_rz] + gd["tp"].get(gp - 4, [])
                                     if s == 0 else [])))
            gd["pn"][gp] = t_pn

        def tp_group(gl):
            first = [gd["pn"][32 * a + 2 * gl][-1],
                     gd["pn"][32 * a + 2 * gl + 1][-1]]
            if gd["atw"][0] is not None:
                first = first + gd["atw"][0]
            if 32 * a + 2 * gl == 0:
                first.append(t_cAll)
            tlast = None
            for i in (0, 1):
                gp = 32 * a + 2 * gl + i
                for s in range(2):
                    col = (2 * (gp % 2) + s) * 128
                    E("tensor",
                      lambda e, src=p_n[gp % 4], ss=s, cl=col:
                          e.transpose(pb[6][:, cl:cl + 128],
                                      src[:, ss * 134:ss * 134 + 128],
                                      idn_sb[:]),
                      waits=first if (i == 0 and s == 0) else [])
                    tlast = E("tensor",
                              lambda e, src=p_n[gp % 4], ss=s, cl=col:
                                  e.transpose(pb[6][0:6, 512 + cl:512 + cl + 128],
                                              src[:, ss * 134 + 128:
                                                  ss * 134 + 134],
                                              idn_sb[:]))
                gd["tp"][gp] = [tlast]
            return tlast

        def at_evac(gl, t_tp):
            gg = 16 * a + gl
            par = gg % 2
            w = [t_tp] + gd["avm"].get(gg - 2, [])
            tc1 = E("scalar",
                    lambda e, d=at1[par]:
                        e.activation(d[:], pb[6][:, 0:512], AF.Identity),
                    waits=w)
            tc2 = E("scalar",
                    lambda e, d=at2[par]:
                        e.activation(d[0:6, :], pb[6][0:6, 512:1024],
                                     AF.Identity),
                    waits=[tc1])
            gd["atw"][0] = [tc1, tc2]
            gd["att"][gg % 2] = [tc1, tc2]
            return [tc1, tc2]

        def av_pair(pr, t_at):
            gp = 32 * a + pr
            gg = gp // 2
            jl, hp = pr // 8, pr % 8
            par = gp % 2
            apar = gg % 2
            bi = 7 if gp % 2 == 0 else 1
            toks = []
            for s in range(2):
                hd = hp * 2 + s
                col = (2 * par + s) * 128
                r0 = s * 64
                tpos = (0, 64) if s == 1 else None
                toks.append(E("tensor",
                              lambda e, t=pbf[bi], jj=jl, hh=hd, cl=col, r=r0,
                                     tp=tpos, ab=at1[apar]:
                                  e.matmul(t[r:r + 64, 0:128],
                                           v_tok[:, jj * H + hh * 64:
                                                 jj * H + hh * 64 + 64],
                                           ab[:, cl:cl + 128],
                                           start=True, stop=False,
                                           tile_position=tp,
                                           skip_group_check=True),
                              waits=([bank_free[bi]] + t_at + t_vev
                                     if s == 0 else [])))
                toks.append(E("tensor",
                              lambda e, t=pbf[bi], jj=jl, hh=hd, cl=col, r=r0,
                                     tp=tpos, ab=at2[apar]:
                                  e.matmul(t[r:r + 64, 0:128],
                                           v_tok[0:6, (jj + 1) * H + hh * 64:
                                                 (jj + 1) * H + hh * 64 + 64],
                                           ab[0:6, cl:cl + 128],
                                           start=False, stop=True,
                                           tile_position=tp,
                                           skip_group_check=True)))
            bank_done[bi] = toks[-1]
            gd["avm"][gg] = gd["avm"].get(gg, []) + [toks[-1]]
            gd[("avmm", gp)] = toks
            bank_free[bi] = E("scalar",
                              lambda e, t=pbf[bi], c=hp, jj=jl:
                                  e.activation(o_fm[c][:, jj * 128:
                                                       jj * 128 + 128],
                                               t[:, 0:128], AF.Identity),
                              waits=[bank_done[bi]] + list(gd["ofm"]))
            gd[("oev", gp)] = bank_free[bi]

        def xres_prefetch(jl):
            gqb = 4 * a + jl
            par = jl % 2
            xw = list(gd["xres"][par])
            if gd["first_xres"]:
                xw += list(t_ln_all)
                gd["first_xres"] = False
            gd[("xr", jl)] = E("sync",
                               lambda e, g=gqb, d=x_res[par]:
                                   e.dma_start(d[:],
                                               x_tok_h[3 + g * 128:
                                                       3 + g * 128 + 128, :]),
                               waits=xw, stream=f"dR{par}")

        def oproj(jl):
            gqb = 4 * a + jl
            par = jl % 2
            oev = [gd[("oev", 32 * a + 8 * jl + k)] for k in range(8)]
            t_xr = gd[("xr", jl)]
            accts = []
            for half in range(2):
                bi = 0
                for ic in range(8):
                    E("tensor",
                      lambda e, i=ic, t=pbf[bi], jj=jl, hh=half:
                          e.matmul(t[:], o_fm[i][:, jj * 128:jj * 128 + 128],
                                   wo_sb[i][:, hh * 512:(hh + 1) * 512],
                                   start=(i == 0), stop=False,
                                   skip_group_check=True),
                      waits=([bank_free[bi]] + oev + t_beff + [t_cAll]
                             if ic == 0 else []))
                tk = E("tensor",
                       lambda e, t=pbf[bi], hh=half:
                           e.matmul(t[:], ones_sb[0:1, :],
                                    beff_sb[0:1, hh * 512:(hh + 1) * 512],
                                    start=False, stop=True,
                                    skip_group_check=True),
                       waits=[])
                bank_done[bi] = tk
                tv = E("vector",
                       lambda e, t=pbf[bi], g=gqb, hh=half, xr=x_res[par]:
                           e.tensor_tensor(acc[:, g * H + hh * 512:
                                               g * H + hh * 512 + 512],
                                           t[:], xr[:, hh * 512:hh * 512 + 512],
                                           ALU.add),
                       waits=[bank_done[bi], t_xr])
                bank_free[bi] = tv
                accts.append(tv)
            gd["xres"][par] = accts
            gd[("acc", gqb)] = accts

        if upto in ("sm", "tp", "av"):
            sc_pair(0)
            sm_pair(0)
            sc_pair(1)
            sm_pair(1)
            if upto in ("tp", "av"):
                t_tp = tp_group(0)
                t_at = at_evac(0, t_tp)
                if upto == "av":
                    av_pair(0, t_at)
                    av_pair(1, t_at)
            return
        for g in range(18):
            if 1 <= g <= 16:
                gl = g - 1
                t_tp = tp_group(gl)
                at_evac(gl, t_tp)
            if g < 16:
                sc_pair(2 * g)
                sm_pair(2 * g)
                sc_pair(2 * g + 1)
                sm_pair(2 * g + 1)
            if g >= 2:
                ga = g - 2
                if ga % 4 == 0:
                    xres_prefetch(ga // 4)
                t_at = gd["att"][ga % 2]
                av_pair(2 * ga, t_at)
                av_pair(2 * ga + 1, t_at)
                if (2 * ga + 1) % 8 == 7:
                    oproj((2 * ga + 1) // 8)

        gd["qkv"] = [p.last("tensor")]
        gd["ofm"] = [p.last("tensor")]

    if stage in ("ln", "ln0"):
        dump_and_finish("xn")
    if stage in ("a1", "a2", "a3", "a4"):
        emit_atile(0, {"a1": "qkv", "a2": "sm", "a3": "tp", "a4": "av"}[stage])
        base = p.barrier_tokens()
        dumps = []
        if stage == "a1":
            for ic in range(8):
                dumps.append((y_h[ic * 128:(ic + 1) * 128, 0:512], q_fm[ic][:]))
                dumps.append((y_h[ic * 128:(ic + 1) * 128, 512:1024],
                              k_fm[ic][:, 0:512]))
            for tb in range(5):
                dumps.append((y_h[1024 + tb * 128:1152 + tb * 128, :],
                              v_tok[:, tb * H:(tb + 1) * H]))
        elif stage == "a2":
            dumps.append((y_h[0:128, 0:268], s_m[0][:]))
            dumps.append((y_h[128:256, 0:268], p_sb[0][:]))
            dumps.append((y_h[256:384, 0:268], p_n[0][:]))
            dumps.append((y_h[384:512, 0:2], nm_sb[0][:]))
            dumps.append((y_h[512:640, 0:2], z_sb[0][:]))
            dumps.append((y_h[640:768, 0:2], rz_sb[0][:]))
            dumps.append((y_h[768:896, 0:268], s_m[1][:]))
            dumps.append((y_h[896:1024, 0:268], p_n[1][:]))
        elif stage == "a3":
            dumps.append((y_h[0:128, 0:512], at1[0][:]))
            dumps.append((y_h[128:134, 0:512], at2[0][0:6, :]))
        elif stage == "a4":
            dumps.append((y_h[0:128, 0:512], o_fm[0][:]))
            dumps.append((y_h[128:256, 0:512], o_fm[1][:]))
        tprev = None
        for dst, srcap in dumps:
            tprev = E("gpsimd", lambda e, d=dst, s=srcap: e.dma_start(d, s),
                      waits=list(base) + ([tprev] if tprev else []),
                      stream="dY0")
        t_last_y[0] = tprev
    if stage in ("full", "attn", "b1", "b2g"):
        for a in range(NA):
            emit_atile(a)
    if stage == "attn":
        dump_and_finish("acc")

    t_phA = p.barrier_tokens()

    g_wg = [[], []]
    g_wu = [[], []]
    hm_prev = [None, None, None, None]
    nhc = (HC if stage in ("full", "b1", "b2g")
           else (1 if stage == "b1g" else 0))
    if nhc:
        E("scalar",
          lambda e: e.add_instruction(mybir.InstLoadActFuncSet(
              name=nc.get_next_instruction_name(),
              act_func_set_id=18, ins=[], outs=[])),
          waits=t_phA)
    for hc in range(nhc):
        par = hc % 2
        t_wg = E("gpsimd",
                 lambda e, c=hc, d=wgst[par]:
                     e.dma_start(d[:], wg_hh[c * 128:(c + 1) * 128, :]),
                 waits=list(g_wg[par]) + (t_phA if hc == 0 else []),
                 stream=f"dG{par}")
        t_wu = E("gpsimd",
                 lambda e, c=hc, d=wust[par]:
                     e.dma_start(d[:], wu_hh[c * 128:(c + 1) * 128, :]),
                 waits=list(g_wu[par]), stream=f"dU{par}")
        gate_last = []
        for ic in range(8):
            for g in range(4):
                bi = g
                tk = E("tensor",
                       lambda e, i=ic, gg=g, t=pbf[bi], w=wgst[par]:
                           e.matmul(t[:], w[:, i * 128:(i + 1) * 128],
                                    xn[i][:, 3 + gg * 512:3 + gg * 512 + 512],
                                    start=(i == 0), stop=(i == 7),
                                    skip_group_check=True),
                       waits=(([bank_free[bi]] if ic == 0 else []) +
                              ([t_wg] + (t_phA if hc == 0 else [])
                               if (ic == 0 and g == 0) else [])))
                if ic == 7:
                    bank_done[bi] = tk
                    gate_last.append(tk)
        g_wg[par] = [gate_last[-1]]
        t_silu = []
        for g in range(4):
            bi = g
            ts_ = E("scalar",
                    lambda e, gg=g, t=pbf[bi]:
                        e.activation(htmp[:, gg * 512:gg * 512 + 512], t[:],
                                     AF.Silu, scale=QS),
                    waits=[bank_done[bi], hm_prev[g]] +
                          (t_phA if hc == 0 and g == 0 else []))
            bank_free[bi] = ts_
            t_silu.append(ts_)
        up_last = []
        for ic in range(8):
            for g in range(4):
                bi = 4 + g
                tk = E("tensor",
                       lambda e, i=ic, gg=g, t=pbf[bi], w=wust[par]:
                           e.matmul(t[:], w[:, i * 128:(i + 1) * 128],
                                    xn[i][:, 3 + gg * 512:3 + gg * 512 + 512],
                                    start=(i == 0), stop=(i == 7),
                                    skip_group_check=True),
                       waits=(([bank_free[bi]] if ic == 0 else []) +
                              ([t_wu] if (ic == 0 and g == 0) else [])))
                if ic == 7:
                    bank_done[bi] = tk
                    up_last.append(tk)
        g_wu[par] = [up_last[-1]]
        for g in range(4):
            bi = 4 + g
            bank_free[bi] = E("vector",
                              lambda e, c=hc, gg=g, t=pbf[bi]:
                                  e.tensor_tensor(
                                      h_sb[c][:, gg * 512:gg * 512 + 512],
                                      htmp[:, gg * 512:gg * 512 + 512],
                                      t[:], ALU.mult),
                              waits=[bank_done[bi], t_silu[g]] +
                                    (t_phA if hc == 0 and g == 0 else []))
            hm_prev[g] = bank_free[bi]

    if stage == "b1":
        dump_and_finish("h")
    if stage == "b1g":
        base = p.barrier_tokens()
        dumps = [(0, wgst[0][:, 0:1024]),
                 (1, htmp[:, 0:1024]),
                 (2, h_sb[0][:, 0:1024]),
                 (3, xn[0][:, 0:1024]),
                 (4, pbf[0][:, 0:512]),
                 (5, pbf[4][:, 0:512])]
        tprev = None
        for row, srcap in dumps:
            tv = E("vector",
                   lambda e, s=srcap, n=srcap.shape[-1]:
                       e.tensor_copy(y_sb[0][:, 0:n], s),
                   waits=list(base) + ([tprev] if tprev else []))
            tprev = E("sync",
                      lambda e, rr=row: e.dma_start(
                          y_h[rr * 128:(rr + 1) * 128, :], y_sb[0][:]),
                      waits=[tv], stream="dY0")
        t_last_y[0] = tprev

    t_phB = p.barrier_tokens()

    t_wd = []
    for hc in range(HC if stage in ("full", "b2g") else 0):
        t_wd.append(E("gpsimd",
                      lambda e, c=hc:
                          e.dma_start(wdT[c][:],
                                      wd_hh[c * 128:(c + 1) * 128, :]),
                      waits=(t_phB if hc == 0 else []),
                      stream="dD"))

    t_ystore = [None, None]
    for tb in range(NQB if stage == "full" else (1 if stage == "b2g" else 0)):
        evs = []
        for half in range(2):
            bi = (tb % 2) * 2 + half
            tk = None
            for hc in range(HC):
                w = []
                if hc == 0:
                    w = [bank_free[bi], t_wd[-1]] + (t_phB
                                                     if tb == 0 and half == 0
                                                     else [])
                tk = E("tensor",
                       lambda e, c=hc, tt=tb, hh=half, t=pbf[bi]:
                           e.matmul(t[:], h_sb[c][:, tt * 128:tt * 128 + 128],
                                    wdT[c][:, hh * 512:(hh + 1) * 512],
                                    start=(c == 0), stop=(c == HC - 1),
                                    skip_group_check=True),
                       waits=w)
            bank_done[bi] = tk
            tv = E("vector",
                   lambda e, tt=tb, hh=half, t=pbf[bi], d=y_sb[tb % 2]:
                       e.scalar_tensor_tensor(
                           d[:, hh * 512:hh * 512 + 512], t[:], QS2,
                           acc[:, tt * H + hh * 512:tt * H + hh * 512 + 512],
                           ALU.mult, ALU.add),
                   waits=[bank_done[bi]] +
                         ([t_ystore[tb % 2]] if t_ystore[tb % 2] else []))
            bank_free[bi] = tv
            evs.append(tv)
        t_ystore[tb % 2] = E("sync",
                             lambda e, tt=tb, s=y_sb[tb % 2]:
                                 e.dma_start(y_h[tt * 128:tt * 128 + 128, :],
                                             s[:]),
                             waits=evs, stream=f"dY{tb % 2}")
        if tb == NQB - 1 or stage == "b2g":
            t_last_y[0] = t_ystore[tb % 2]
    if stage == "b2g":
        base = p.barrier_tokens()
        tprev = None
        for row, srcap in ((1, wdT[0][:, 0:1024]), (2, wdT[23][:, 0:1024]),
                           (3, acc[:, 0:1024]), (4, h_sb[0][:, 0:1024])):
            tv = E("vector",
                   lambda e, s=srcap: e.tensor_copy(y_sb[1][:], s),
                   waits=list(base) + ([tprev] if tprev else []))
            tprev = E("sync",
                      lambda e, rr=row: e.dma_start(
                          y_h[rr * 128:(rr + 1) * 128, :], y_sb[1][:]),
                      waits=[tv], stream="dY1")
        t_last_y[0] = tprev

    final_tok = t_last_y[0]
    all_keys = set(p.nkey)
    dma_keys = all_keys - set(("tensor", "vector", "scalar"))
    waited = {k: set() for k in all_keys}
    for k in all_keys:
        if k not in dma_keys:
            waited[k] = set(range(p.nkey[k]))
    for e in ENGINES:
        for _, ws, _k in p.ops[e]:
            for (wk_, wi) in ws:
                waited[wk_].add(wi)
    waited[final_tok[0]].add(final_tok[1])
    rank = {}
    for k in all_keys:
        if k in dma_keys:
            rank[k] = {i: i + 1 for i in range(p.nkey[k])}
        else:
            rank[k] = {i: r + 1 for r, i in enumerate(sorted(waited[k]))}
    unit = {k: (16 if k in dma_keys else 1) for k in all_keys}

    import contextlib
    with contextlib.ExitStack() as stk:
        semh = {k: stk.enter_context(nc.semaphore("c_" + str(k)))
                for k in all_keys}
        kcnt = {k: 0 for k in all_keys}
        with nc.Block() as block:
            def make(eng_name):
                def run(e):
                    if eng_name == "scalar":
                        e.add_instruction(mybir.InstLoadActFuncSet(
                            name=nc.get_next_instruction_name(),
                            act_func_set_id=6, ins=[], outs=[]))
                    for fn, ws, key in p.ops[eng_name]:
                        for (wk_, wi) in ws:
                            e.wait_ge(semh[wk_], rank[wk_][wi] * unit[wk_])
                        inst = fn(e)
                        i = kcnt[key]
                        kcnt[key] += 1
                        if key in dma_keys or i in waited[key]:
                            inst.then_inc(semh[key], unit[key])
                    if eng_name == "sync":
                        e.wait_ge(semh[final_tok[0]],
                                  rank[final_tok[0]][final_tok[1]] *
                                  unit[final_tok[0]])
                return run

            block.sync(make("sync"))
            block.gpsimd(make("gpsimd"))
            block.tensor(make("tensor"))
            block.vector(make("vector"))
            block.scalar(make("scalar"))
    return nc


_NC = None


def _get_nc():
    global _NC
    if _NC is None:
        _NC = build_nc()
    return _NC


def _band_mask(first, last):
    m = np.full((128, 134), NEG, np.float32)
    for r in range(128):
        lo = max(r, 3) if first else r
        hi = min(r + 6, 130) if last else r + 6
        m[r, lo:hi + 1] = 0.0
    return np.concatenate([m, m], axis=1).astype(ml_dtypes.bfloat16)


def kernel(L0_attn_hidden_states, L0_attn_w_q, L0_attn_b_q, L0_attn_w_k,
           L0_attn_b_k, L0_attn_w_v, L0_attn_b_v, L0_attn_w_o, L0_attn_b_o,
           L0_ffn_w_gate, L0_ffn_w_up, L0_ffn_w_down):
    x = np.ascontiguousarray(np.asarray(L0_attn_hidden_states, np.float32))
    wq = np.ascontiguousarray(np.asarray(L0_attn_w_q, np.float32))
    wk = np.ascontiguousarray(np.asarray(L0_attn_w_k, np.float32))
    wv = np.ascontiguousarray(np.asarray(L0_attn_w_v, np.float32))
    wo = np.ascontiguousarray(np.asarray(L0_attn_w_o, np.float32))
    bq = np.asarray(L0_attn_b_q, np.float32)
    bk = np.asarray(L0_attn_b_k, np.float32)
    bv = np.asarray(L0_attn_b_v, np.float32)
    bo = np.asarray(L0_attn_b_o, np.float32)
    wg = np.asarray(L0_ffn_w_gate, np.int32)
    wu = np.asarray(L0_ffn_w_up, np.int32)
    wd = np.asarray(L0_ffn_w_down, np.int32)

    wg_b = np.ascontiguousarray(
        wg.reshape(HC, 128, 8, 128).transpose(0, 3, 2, 1).reshape(FH, H))
    wu_b = np.ascontiguousarray(
        wu.reshape(HC, 128, 8, 128).transpose(0, 3, 2, 1).reshape(FH, H))
    wd_b = np.ascontiguousarray(wd.T)

    shared = dict(
        wq=wq, wk=wk, wv=wv, wo=wo,
        bq=np.ascontiguousarray(bq.reshape(8, 128).T),
        bk=np.ascontiguousarray(bk.reshape(8, 128).T),
        bv=np.ascontiguousarray(bv.reshape(8, 128).T),
        bo_row=np.ascontiguousarray(bo.reshape(1, H)),
        wg_b=wg_b, wu_b=wu_b, wd_b=wd_b,
        ident=np.eye(128, dtype=ml_dtypes.bfloat16),
        ones_row=np.ones((1, 128), dtype=ml_dtypes.bfloat16),
        eps_col=np.full((128, 1), EPS, np.float32),
    )
    m_mid = _band_mask(False, False)
    m_first = _band_mask(True, False)
    m_last = _band_mask(False, True)

    in_maps = []
    for core in range(NCORES):
        b_i, qi = core // 4, core % 4
        t0 = qi * TOK
        x_ext = np.zeros((EXT, H), np.float32)
        lo = t0 - 3
        s0, s1 = max(0, lo), min(S, lo + EXT)
        x_ext[s0 - lo:s1 - lo] = x[b_i, s0:s1]
        im = dict(shared)
        im["x_tok"] = x_ext
        im["mask_first"] = m_first if qi == 0 else m_mid
        im["mask_mid"] = m_mid
        im["mask_last"] = m_last if qi == 3 else m_mid
        in_maps.append(im)

    res = run_bass_kernel_spmd(_get_nc(), in_maps, core_ids=list(range(NCORES)))
    y = np.empty((B, S, H), np.float32)
    for core in range(NCORES):
        b_i, qi = core // 4, core % 4
        y[b_i, qi * TOK:(qi + 1) * TOK] = res.results[core]["y"]
    return y


if __name__ == "__main__":
    np.random.seed(0)
    import reference as ref
    ins = {k: np.asarray(v) for k, v in ref.setup_inputs().items()}
    out = kernel(**ins)
    exp = np.asarray(ref.reference(**ins))
    print("rvr:", float(np.mean((out - exp) ** 2) / np.mean(exp ** 2)))
    print("maxabs:", float(np.abs(out - exp).max()))


# version 14
# speedup vs baseline: 1.0196x; 1.0196x over previous
import numpy as np
import ml_dtypes
import concourse.bass as bass
import concourse.mybir as mybir
from concourse.bass_utils import run_bass_kernel_spmd

F32 = mybir.dt.float32
BF16 = mybir.dt.bfloat16
I32 = mybir.dt.int32
AX = mybir.AxisListType.X
AF = mybir.ActivationFunctionType
ALU = mybir.AluOpType

B, S, H = 2, 8192, 1024
NCORES = 8
TOK = 2048
EXT = 2176
NTILE = 17
SCALE = 0.125
FH, HC = 3072, 24
QS = 1.0 / 127.0
QS2 = QS * QS
NA = 4
NQB = 16
EPS = 1e-5
NEG = -1e30

ENGINES = ("sync", "gpsimd", "tensor", "vector", "scalar")
DMA_ENG = {"sync": True, "gpsimd": True, "tensor": False, "vector": False,
           "scalar": False}


class Prog:

    def __init__(self):
        self.ops = {e: [] for e in ENGINES}
        self.nkey = {}
        self.last_of = {}

    def emit(self, eng, fn, waits=(), stream=None):
        key = stream if stream is not None else eng
        idx = self.nkey.get(key, 0)
        self.nkey[key] = idx + 1
        tok = (key, idx)
        self.ops[eng].append((fn, [w for w in waits if w is not None], key))
        self.last_of[key] = tok
        return tok

    def last(self, eng):
        return self.last_of.get(eng)

    def barrier_tokens(self):
        return list(self.last_of.values())


def build_nc(stage="full"):
    nc = bass.Bass("TRN2", target_bir_lowering=False)
    p = Prog()

    x_tok_h = nc.dram_tensor("x_tok", [EXT, H], F32, kind="ExternalInput").ap()
    wq_h = nc.dram_tensor("wq", [H, H], F32, kind="ExternalInput").ap()
    wk_h = nc.dram_tensor("wk", [H, H], F32, kind="ExternalInput").ap()
    wv_h = nc.dram_tensor("wv", [H, H], F32, kind="ExternalInput").ap()
    wo_h = nc.dram_tensor("wo", [H, H], F32, kind="ExternalInput").ap()
    bq_h = nc.dram_tensor("bq", [128, 8], F32, kind="ExternalInput").ap()
    bk_h = nc.dram_tensor("bk", [128, 8], F32, kind="ExternalInput").ap()
    bv_h = nc.dram_tensor("bv", [128, 8], F32, kind="ExternalInput").ap()
    bo_h = nc.dram_tensor("bo_row", [1, H], F32, kind="ExternalInput").ap()
    wg_hh = nc.dram_tensor("wg_b", [FH, H], I32, kind="ExternalInput").ap()
    wu_hh = nc.dram_tensor("wu_b", [FH, H], I32, kind="ExternalInput").ap()
    wd_hh = nc.dram_tensor("wd_b", [FH, H], I32, kind="ExternalInput").ap()
    mskf_h = nc.dram_tensor("mask_first", [128, 268], BF16, kind="ExternalInput").ap()
    mskm_h = nc.dram_tensor("mask_mid", [128, 268], BF16, kind="ExternalInput").ap()
    mskl_h = nc.dram_tensor("mask_last", [128, 268], BF16, kind="ExternalInput").ap()
    idn_h = nc.dram_tensor("ident", [128, 128], BF16, kind="ExternalInput").ap()
    idf_h = nc.dram_tensor("ident_f", [128, 128], F32, kind="ExternalInput").ap()
    ones_h = nc.dram_tensor("ones_row", [1, 128], BF16, kind="ExternalInput").ap()
    eps_h = nc.dram_tensor("eps_col", [128, 1], F32, kind="ExternalInput").ap()
    y_h = nc.dram_tensor("y", [TOK, H], F32, kind="ExternalOutput").ap()

    SBUF_BASE = 16640
    SBUF_LIMIT = 229376
    _off = [SBUF_BASE]

    def at(name, shape, dtype, off=None):
        nb = shape[-1] * mybir.dt.size(dtype)
        if off is None:
            off = _off[0]
            _off[0] = off + ((nb + 255) // 256) * 256
            assert _off[0] <= SBUF_LIMIT, (name, _off[0])
        return nc.alloc_sbuf_tensor_at(name, list(shape), dtype, offset=off)

    mskf = at("mskf", [128, 268], BF16)
    mskm = at("mskm", [128, 268], BF16)
    mskl = at("mskl", [128, 268], BF16)
    bq_sb = at("bq_sb", [128, 8], F32)
    bk_sb = at("bk_sb", [128, 8], F32)
    bv_sb = at("bv_sb", [128, 8], BF16)
    bqs_sb = at("bqs_sb", [128, 8], F32)
    bo_sb = at("bo_sb", [1, H], F32)
    beff_sb = at("beff_sb", [1, H], BF16)
    idn_sb = at("idn_sb", [128, 128], BF16)
    idf_sb = at("idf_sb", [128, 128], F32)
    ones_sb = at("ones_sb", [1, 128], BF16)
    eps_sb = at("eps_sb", [128, 1], F32)
    st = {k: at("st_" + k, [128, NTILE], F32)
          for k in ("s1", "s2", "ms", "m2", "va", "ln", "a", "b")}
    acc = at("acc", [128, NQB * H], BF16)
    y_sb = [at(f"y_sb{i}", [128, H], F32) for i in range(2)]

    xn0 = _off[0]
    xn = [at(f"xn{ic}", [128, EXT], BF16) for ic in range(8)]
    h0 = _off[0]
    h_sb = [at(f"h{c}", [128, TOK], BF16) for c in range(HC)]
    top0 = _off[0]
    wgst = [at(f"wgst{i}", [128, H], BF16) for i in range(2)]
    wust = [at(f"wust{i}", [128, H], BF16) for i in range(2)]
    htmp = at("htmp", [128, TOK], BF16)
    assert _off[0] <= SBUF_LIMIT

    wdT = []
    for c in range(HC):
        off = xn0 + 2048 * c if c < 17 else top0 + 2048 * (c - 17)
        wdT.append(at(f"wdT{c}", [128, H], BF16, off=off))

    a_off = [h0]

    def atA(name, shape, dtype):
        nb = shape[-1] * mybir.dt.size(dtype)
        off = a_off[0]
        a_off[0] = off + ((nb + 255) // 256) * 256
        assert a_off[0] <= SBUF_LIMIT, (name, a_off[0])
        return nc.alloc_sbuf_tensor_at(name, list(shape), dtype, offset=off)

    wq_sb = [atA(f"wqb{i}", [128, H], BF16) for i in range(8)]
    wk_sb = [atA(f"wkb{i}", [128, H], BF16) for i in range(8)]
    wv_sb = [atA(f"wvb{i}", [128, H], BF16) for i in range(8)]
    wo_sb = [atA(f"wob{i}", [128, H], BF16) for i in range(8)]
    ln0 = a_off[0]
    q_fm = [atA(f"qfm{i}", [128, 512], BF16) for i in range(8)]
    k_fm = [atA(f"kfm{i}", [128, 528], BF16) for i in range(8)]
    v_tok = atA("v_tok", [128, 5 * H], BF16)
    o_fm = [atA(f"ofm{i}", [128, 512], BF16) for i in range(8)]
    p_sb = [atA(f"p_sb{i}", [128, 268], BF16) for i in range(2)]
    p_n = [atA(f"p_n{i}", [128, 268], F32) for i in range(4)]
    nm_sb = [atA(f"nm{i}", [128, 2], F32) for i in range(2)]
    z_sb = [atA(f"z{i}", [128, 2], F32) for i in range(2)]
    rz_sb = [atA(f"rz{i}", [128, 2], F32) for i in range(2)]
    at1 = [atA(f"at1_{i}", [128, 256], BF16) for i in range(4)]
    at2 = [atA(f"at2_{i}", [6, 256], BF16) for i in range(4)]
    x_res = [atA(f"xres{i}", [128, H], F32) for i in range(2)]
    lnoff = [ln0]

    def atLN(name, shape, dtype):
        nb = shape[-1] * mybir.dt.size(dtype)
        off = lnoff[0]
        lnoff[0] = off + ((nb + 255) // 256) * 256
        return nc.alloc_sbuf_tensor_at(name, list(shape), dtype, offset=off)

    xst = [atLN(f"xst{i}", [128, H], F32) for i in range(2)]
    xnst = [atLN(f"xnst{i}", [128, H], F32) for i in range(2)]
    sq_scr = atLN("sq_scr", [128, H], BF16)

    pb = [nc.alloc_psum_tensor(f"pb{i}", [128, 512], F32) for i in range(8)]
    pbf = [t.ap() for t in pb]

    E = p.emit

    t_consts = []
    for dst, src in ((mskf, mskf_h), (mskm, mskm_h), (mskl, mskl_h),
                     (bq_sb, bq_h), (bk_sb, bk_h),
                     (bo_sb, bo_h), (idn_sb, idn_h), (idf_sb, idf_h),
                     (ones_sb, ones_h),
                     (eps_sb, eps_h)):
        t_consts.append(E("sync", lambda e, d=dst, s=src: e.dma_start(d[:], s[:]),
                          stream="dC"))
    t_cAll = t_consts[-1]
    t_bv = E("gpsimd", lambda e: e.dma_start(bv_sb[:], bv_h[:]), stream="dBv")

    t_w = {}
    wlist = (() if stage == "ln0" else
             (("q", wq_sb, wq_h), ("k", wk_sb, wk_h),
              ("v", wv_sb, wv_h), ("o", wo_sb, wo_h)))
    for nm, bufs, src in wlist:
        t = None
        for ic in range(8):
            t = E("gpsimd",
                  lambda e, d=bufs[ic], s=src, i=ic:
                      e.dma_start(d[:], s[i * 128:(i + 1) * 128, :]),
                  stream="dW" + nm)
        t_w[nm] = t

    t_bqs = E("vector", lambda e: e.tensor_scalar_mul(bqs_sb[:], bq_sb[:], SCALE),
              waits=[t_cAll])

    t_beff = []
    bank_free = [None] * 8
    bank_done = [None] * 8
    for half in range(2 if stage != "ln0" else 0):
        tk = None
        for ic in range(8):
            tk = E("tensor",
                   lambda e, i=ic, t=pbf[half], hh=half:
                       e.matmul(t[0:1, :], bv_sb[:, i:i + 1],
                                wo_sb[i][:, hh * 512:(hh + 1) * 512],
                                start=(i == 0), stop=(i == 7),
                                skip_group_check=True),
                   waits=([t_w["o"], t_bv] if ic == 0 else []))
        bank_done[half] = tk
        tv = E("vector",
               lambda e, t=pbf[half], hh=half:
                   e.tensor_tensor(beff_sb[0:1, hh * 512:(hh + 1) * 512],
                                   t[0:1, :], bo_sb[0:1, hh * 512:(hh + 1) * 512],
                                   ALU.add),
               waits=[bank_done[half], t_cAll])
        bank_free[half] = tv
        t_beff.append(tv)

    inv_h = 1.0 / H
    xst_guard = [[], []]
    xnst_guard = [[], []]
    ln_bank_guard = [[], []]
    for i in range(NTILE):
        par = i % 2
        t_x = E("sync",
                lambda e, j=i, d=xst[par]:
                    e.dma_start(d[:], x_tok_h[j * 128:(j + 1) * 128, :]),
                waits=xst_guard[par], stream=f"dX{par}")
        t_s1 = E("vector",
                 lambda e, j=i, s=xst[par]:
                     e.reduce_sum(st["s1"][:, j:j + 1], s[:], axis=AX),
                 waits=[t_x])
        t_sq = E("scalar",
                 lambda e, j=i, s=xst[par]:
                     e.activation(sq_scr[:], s[:], AF.Square,
                                  accum_out=st["s2"][:, j:j + 1]),
                 waits=[t_x])
        t_ms = E("vector",
                 lambda e, j=i:
                     e.tensor_scalar_mul(st["ms"][:, j:j + 1],
                                         st["s1"][:, j:j + 1], inv_h),
                 waits=[t_s1])
        t_m2 = E("vector",
                 lambda e, j=i:
                     e.tensor_mul(st["m2"][:, j:j + 1], st["ms"][:, j:j + 1],
                                  st["ms"][:, j:j + 1]),
                 waits=[t_ms])
        t_va = E("vector",
                 lambda e, j=i:
                     e.scalar_tensor_tensor(st["va"][:, j:j + 1],
                                            st["s2"][:, j:j + 1], inv_h,
                                            st["m2"][:, j:j + 1],
                                            ALU.mult, ALU.subtract),
                 waits=[t_sq, t_m2])
        t_ln = E("scalar",
                 lambda e, j=i:
                     e.activation(st["ln"][:, j:j + 1], st["va"][:, j:j + 1],
                                  AF.Ln, bias=eps_sb[:, 0:1]),
                 waits=[t_va, t_cAll])
        t_a = E("scalar",
                lambda e, j=i:
                    e.activation(st["a"][:, j:j + 1], st["ln"][:, j:j + 1],
                                 AF.Exp, scale=-0.5),
                waits=[t_ln])
        t_b = E("vector",
                lambda e, j=i:
                    e.tensor_scalar(st["b"][:, j:j + 1], st["ms"][:, j:j + 1],
                                    st["a"][:, j:j + 1], -1.0,
                                    ALU.mult, ALU.mult),
                waits=[t_a])
        t_ap = E("scalar",
                 lambda e, j=i, s=xst[par], d=xnst[par]:
                     e.activation(d[:], s[:], AF.Identity,
                                  bias=st["b"][:, j:j + 1],
                                  scale=st["a"][:, j:j + 1]),
                 waits=[t_b] + xnst_guard[par])
        xst_guard[par] = [t_sq, t_ap]
        t_tr = None
        for ic in range(8):
            t_tr = E("tensor",
                     lambda e, j=i, c=ic, s=xnst[par], bk=pbf[6 + (ic // 4)]:
                         e.transpose(bk[:, (c % 4) * 128:(c % 4) * 128 + 128],
                                     s[:, c * 128:(c + 1) * 128], idf_sb[:]),
                     waits=([t_ap, t_cAll] + ln_bank_guard[0]
                            if ic == 0 else []))
        xnst_guard[par] = [t_tr]
        evs = []
        for ic in range(8):
            eng = "vector" if ic < 4 else "scalar"
            fn = ((lambda e, j=i, c=ic, bk=pbf[6]:
                       e.tensor_copy(xn[c][:, j * 128:(j + 1) * 128],
                                     bk[:, (c % 4) * 128:(c % 4) * 128 + 128]))
                  if eng == "vector" else
                  (lambda e, j=i, c=ic, bk=pbf[7]:
                       e.activation(xn[c][:, j * 128:(j + 1) * 128],
                                    bk[:, (c % 4) * 128:(c % 4) * 128 + 128],
                                    AF.Identity)))
            evs.append(E(eng, fn, waits=[t_tr] if ic in (0, 4) else []))
        ln_bank_guard[0] = [evs[3], evs[7]]
    t_ln_all = p.barrier_tokens()

    gd = {"exps": {}, "pn": {}, "tp": {}, "xres": [[], []],
          "qkv": [], "ofm": [], "first_xres": True,
          "avm": {}, "avmp": {}, "atw": [[], []], "att": [None, None],
          "xr": {}}
    t_last_y = [None]

    def dump_and_finish(kind):
        tprev = None
        base = p.barrier_tokens()
        if kind == "xn":
            for ki, kk in enumerate(("s1", "s2", "ms", "m2", "va", "ln", "a", "b")):
                tprev = E("sync",
                          lambda e, k=ki, t=st[kk]:
                              e.dma_start(y_h[1024 + 128 * k:1152 + 128 * k, 0:NTILE],
                                          t[:]),
                          waits=list(base) + ([tprev] if tprev else []),
                          stream="dY0")
        n = 8 if kind == "xn" else NQB
        for i in range(n):
            w = list(base) + ([tprev] if tprev else [])
            if kind == "xn":
                tv = E("vector",
                       lambda e, c=i, d=y_sb[i % 2]:
                           e.tensor_copy(d[:], xn[c][:, 0:H]), waits=w)
            elif kind == "acc":
                tv = E("vector",
                       lambda e, t=i, d=y_sb[i % 2]:
                           e.tensor_copy(d[:], acc[:, t * H:(t + 1) * H]),
                       waits=w)
            else:
                tv = E("vector",
                       lambda e, c=i, d=y_sb[i % 2]:
                           e.tensor_copy(d[:], h_sb[c][:, 0:H]), waits=w)
            tprev = E("sync",
                      lambda e, t=i, s=y_sb[i % 2]:
                          e.dma_start(y_h[t * 128:(t + 1) * 128, :], s[:]),
                      waits=[tv], stream=f"dY{i % 2}")
        t_last_y[0] = tprev

    def emit_atile(a, upto="full"):
        ext0 = 512 * a
        ln_wait = list(t_ln_all) if a == 0 else []

        for jc in range(8):
            bi = jc % 2
            tk = None
            for ic in range(8):
                tk = E("tensor",
                       lambda e, i=ic, j=jc, t=pbf[bi], e0=ext0:
                           e.matmul(t[:], wq_sb[i][:, j * 128:(j + 1) * 128],
                                    xn[i][:, e0 + 3:e0 + 515],
                                    start=(i == 0), stop=(i == 7),
                                    skip_group_check=True),
                       waits=([bank_free[bi], t_w["q"], t_bqs] + ln_wait
                              if ic == 0 else []))
            bank_done[bi] = tk
            bank_free[bi] = E("scalar",
                              lambda e, j=jc, t=pbf[bi]:
                                  e.activation(q_fm[j][:], t[:], AF.Identity,
                                               bias=bqs_sb[:, j:j + 1],
                                               scale=SCALE),
                              waits=[bank_done[bi]] + list(gd["qkv"]))
        t_qev = [bank_free[0], bank_free[1]]
        for jc in range(8):
            for grp, (c0, cw) in enumerate(((0, 512), (512, 6))):
                bi = grp
                tk = None
                for ic in range(8):
                    tk = E("tensor",
                           lambda e, i=ic, j=jc, t=pbf[bi], e0=ext0,
                                  cc=c0, ww=cw:
                               e.matmul(t[:, 0:ww],
                                        wk_sb[i][:, j * 128:(j + 1) * 128],
                                        xn[i][:, e0 + cc:e0 + cc + ww],
                                        start=(i == 0), stop=(i == 7),
                                        skip_group_check=True),
                           waits=([bank_free[bi], t_w["k"], t_cAll]
                                  + ln_wait if ic == 0 else []))
                bank_done[bi] = tk
                bank_free[bi] = E("scalar",
                                  lambda e, j=jc, t=pbf[bi], cc=c0, ww=cw:
                                      e.activation(k_fm[j][:, cc:cc + ww],
                                                   t[:, 0:ww], AF.Identity,
                                                   bias=bk_sb[:, j:j + 1]),
                                  waits=[bank_done[bi]] + list(gd["qkv"]))
        t_kev = [bank_free[0], bank_free[1]]
        for tb in range(5):
            for half in range(2):
                bi = half
                tk = None
                for ic in range(8):
                    tk = E("tensor",
                           lambda e, i=ic, t=pbf[bi], m=4 * a + tb, hh=half:
                               e.matmul(t[:],
                                        xn[i][:, m * 128:m * 128 + 128],
                                        wv_sb[i][:, hh * 512:(hh + 1) * 512],
                                        start=(i == 0), stop=(i == 7),
                                        skip_group_check=True),
                           waits=([bank_free[bi], t_w["v"]] + ln_wait
                                  if ic == 0 else []))
                bank_done[bi] = tk
                bank_free[bi] = E("scalar",
                                  lambda e, t=pbf[bi], s=tb, hh=half:
                                      e.activation(
                                          v_tok[:, s * H + hh * 512:
                                                s * H + hh * 512 + 512],
                                          t[:], AF.Identity),
                                  waits=[bank_done[bi]] + list(gd["qkv"]))
        t_vev = [bank_free[0], bank_free[1]]
        t_qkv = t_qev + t_kev + t_vev
        if upto == "qkv":
            return

        def msel(jl):
            g = 4 * a + jl
            return mskf if g == 0 else (mskl if g == NQB - 1 else mskm)

        def sc_pair(pr):
            gp = 32 * a + pr
            jl, hp = pr // 8, pr % 8
            m = msel(pr // 8)
            toks = []
            for s in range(2):
                bi = (2 if s == 0 else 4) + gp % 2
                hd = hp * 2 + s
                jc, off = hd // 2, (hd % 2) * 64
                toks.append(E("tensor",
                              lambda e, t=pbf[bi], jj=jl, c=jc, o=off:
                                  e.matmul(t[:, 0:134],
                                           q_fm[c][o:o + 64,
                                                   jj * 128:jj * 128 + 128],
                                           k_fm[c][o:o + 64,
                                                   jj * 128:jj * 128 + 134],
                                           start=True, stop=False,
                                           skip_group_check=True),
                              waits=[bank_free[bi]] + (t_qkv if s == 0 else [])))
                toks.append(E("tensor",
                              lambda e, t=pbf[bi], mm=m, ss=s:
                                  e.matmul(t[:, 0:134], idn_sb[:],
                                           mm[:, ss * 134:ss * 134 + 134],
                                           start=False, stop=True,
                                           skip_group_check=True),
                              waits=[t_cAll] if gp < 10 else []))
                bank_done[bi] = toks[-1]
            gd[("scmm", gp)] = toks

        def sm_pair(pr):
            gp = 32 * a + pr
            par = gp % 2
            t_exps = []
            for s in range(2):
                sbi = (2 if s == 0 else 4) + gp % 2
                t_nm = E("vector",
                         lambda e, t=pbf[sbi], d=nm_sb[par], ss=s:
                             e.reduce_max(d[:, ss:ss + 1], t[:, 0:134],
                                          axis=AX, negate=True),
                         waits=[bank_done[sbi]] +
                               (gd["exps"].get(gp - 2, []) if s == 0 else []))
                t_exps.append(E("scalar",
                                lambda e, t=pbf[sbi], ss=s, d=p_sb[par],
                                       nn=nm_sb[par], zz=z_sb[par]:
                                    e.activation(d[:, ss * 134:ss * 134 + 134],
                                                 t[:, 0:134],
                                                 AF.Exp, bias=nn[:, ss:ss + 1],
                                                 accum_out=zz[:, ss:ss + 1]),
                                waits=[t_nm] +
                                      (gd["pn"].get(gp - 2, [])
                                       if s == 0 else [])))
                bank_free[sbi] = t_exps[-1]
            gd["exps"][gp] = t_exps
            t_rz = E("vector",
                     lambda e, zz=z_sb[par], rr=rz_sb[par]:
                         e.reciprocal(rr[:], zz[:]),
                     waits=list(t_exps))
            t_pn = []
            for s in range(2):
                t_pn.append(E("vector",
                              lambda e, ss=s, src=p_sb[par], d=p_n[gp % 4],
                                     rr=rz_sb[par]:
                                  e.tensor_scalar_mul(
                                      d[:, ss * 134:ss * 134 + 134],
                                      src[:, ss * 134:ss * 134 + 134],
                                      rr[:, ss:ss + 1]),
                              waits=([t_rz] + gd["tp"].get(gp - 4, [])
                                     if s == 0 else [])))
            gd["pn"][gp] = t_pn

        def tp_group(gl):
            tlast = None
            for i in (0, 1):
                gp = 32 * a + 2 * gl + i
                tbk = 6 if gp % 2 == 0 else 1
                first = [gd["pn"][gp][-1]] + gd["atw"][gp % 2]
                if gp == 0:
                    first.append(t_cAll)
                for s in range(2):
                    E("tensor",
                      lambda e, src=p_n[gp % 4], ss=s, bk=pbf[tbk]:
                          e.transpose(bk[:, ss * 128:ss * 128 + 128],
                                      src[:, ss * 134:ss * 134 + 128],
                                      idf_sb[:]),
                      waits=first if s == 0 else [])
                    tlast = E("tensor",
                              lambda e, src=p_n[gp % 4], ss=s, bk=pbf[tbk]:
                                  e.transpose(bk[0:6, 256 + ss * 128:
                                                 256 + ss * 128 + 128],
                                              src[:, ss * 134 + 128:
                                                  ss * 134 + 134],
                                              idf_sb[:]))
                gd["tp"][gp] = [tlast]
                gd[("tpd", gp)] = tlast
            return tlast

        def at_evac(gl, t_tp):
            toks = []
            for i in (0, 1):
                gp = 32 * a + 2 * gl + i
                tbk = 6 if gp % 2 == 0 else 1
                w = [gd[("tpd", gp)]] + gd["avmp"].get(gp - 4, [])
                tc1 = E("scalar",
                        lambda e, d=at1[gp % 4], bk=pbf[tbk]:
                            e.activation(d[:], bk[:, 0:256], AF.Identity),
                        waits=w)
                tc2 = E("scalar",
                        lambda e, d=at2[gp % 4], bk=pbf[tbk]:
                            e.activation(d[0:6, :], bk[0:6, 256:512],
                                         AF.Identity),
                        waits=[tc1])
                gd["atw"][gp % 2] = [tc1, tc2]
                bank_free[tbk] = tc2
                gd[("atp", gp)] = [tc1, tc2]
            return toks

        def av_pair(pr, t_at):
            gp = 32 * a + pr
            gg = gp // 2
            jl, hp = pr // 8, pr % 8
            bi = 7 if gp % 2 == 0 else 0
            toks = []
            for s in range(2):
                hd = hp * 2 + s
                col = s * 128
                r0 = s * 64
                tpos = (0, 64) if s == 1 else None
                toks.append(E("tensor",
                              lambda e, t=pbf[bi], jj=jl, hh=hd, cl=col, r=r0,
                                     tp=tpos, ab=at1[gp % 4]:
                                  e.matmul(t[r:r + 64, 0:128],
                                           v_tok[:, jj * H + hh * 64:
                                                 jj * H + hh * 64 + 64],
                                           ab[:, cl:cl + 128],
                                           start=True, stop=False,
                                           tile_position=tp,
                                           skip_group_check=True),
                              waits=([bank_free[bi]] + gd[("atp", gp)] + t_vev
                                     if s == 0 else [])))
                toks.append(E("tensor",
                              lambda e, t=pbf[bi], jj=jl, hh=hd, cl=col, r=r0,
                                     tp=tpos, ab=at2[gp % 4]:
                                  e.matmul(t[r:r + 64, 0:128],
                                           v_tok[0:6, (jj + 1) * H + hh * 64:
                                                 (jj + 1) * H + hh * 64 + 64],
                                           ab[0:6, cl:cl + 128],
                                           start=False, stop=True,
                                           tile_position=tp,
                                           skip_group_check=True)))
            bank_done[bi] = toks[-1]
            gd["avmp"][gp] = [toks[-1]]
            gd[("avmm", gp)] = toks
            bank_free[bi] = E("scalar",
                              lambda e, t=pbf[bi], c=hp, jj=jl:
                                  e.activation(o_fm[c][:, jj * 128:
                                                       jj * 128 + 128],
                                               t[:, 0:128], AF.Identity),
                              waits=[bank_done[bi]] + list(gd["ofm"]))
            gd[("oev", gp)] = bank_free[bi]

        def xres_prefetch(jl):
            gqb = 4 * a + jl
            par = jl % 2
            xw = list(gd["xres"][par])
            if gd["first_xres"]:
                xw += list(t_ln_all)
                gd["first_xres"] = False
            gd[("xr", jl)] = E("sync",
                               lambda e, g=gqb, d=x_res[par]:
                                   e.dma_start(d[:],
                                               x_tok_h[3 + g * 128:
                                                       3 + g * 128 + 128, :]),
                               waits=xw, stream=f"dR{par}")

        def oproj(jl):
            gqb = 4 * a + jl
            par = jl % 2
            oev = [gd[("oev", 32 * a + 8 * jl + k)] for k in range(8)]
            t_xr = gd[("xr", jl)]
            accts = []
            for half in range(2):
                bi = 0
                for ic in range(8):
                    E("tensor",
                      lambda e, i=ic, t=pbf[bi], jj=jl, hh=half:
                          e.matmul(t[:], o_fm[i][:, jj * 128:jj * 128 + 128],
                                   wo_sb[i][:, hh * 512:(hh + 1) * 512],
                                   start=(i == 0), stop=False,
                                   skip_group_check=True),
                      waits=([bank_free[bi]] + oev + t_beff + [t_cAll]
                             if ic == 0 else []))
                tk = E("tensor",
                       lambda e, t=pbf[bi], hh=half:
                           e.matmul(t[:], ones_sb[0:1, :],
                                    beff_sb[0:1, hh * 512:(hh + 1) * 512],
                                    start=False, stop=True,
                                    skip_group_check=True),
                       waits=[])
                bank_done[bi] = tk
                tv = E("vector",
                       lambda e, t=pbf[bi], g=gqb, hh=half, xr=x_res[par]:
                           e.tensor_tensor(acc[:, g * H + hh * 512:
                                               g * H + hh * 512 + 512],
                                           t[:], xr[:, hh * 512:hh * 512 + 512],
                                           ALU.add),
                       waits=[bank_done[bi], t_xr])
                bank_free[bi] = tv
                accts.append(tv)
            gd["xres"][par] = accts
            gd[("acc", gqb)] = accts

        if upto in ("sm", "tp", "av"):
            sc_pair(0)
            sm_pair(0)
            sc_pair(1)
            sm_pair(1)
            if upto in ("tp", "av"):
                t_tp = tp_group(0)
                t_at = at_evac(0, t_tp)
                if upto == "av":
                    av_pair(0, t_at)
                    av_pair(1, t_at)
            return
        for g in range(18):
            if 1 <= g <= 16:
                gl = g - 1
                t_tp = tp_group(gl)
                at_evac(gl, t_tp)
            if g < 16:
                sc_pair(2 * g)
                sm_pair(2 * g)
                sc_pair(2 * g + 1)
                sm_pair(2 * g + 1)
            if g >= 2:
                ga = g - 2
                if ga % 4 == 0:
                    xres_prefetch(ga // 4)
                av_pair(2 * ga, None)
                av_pair(2 * ga + 1, None)
                if (2 * ga + 1) % 8 == 7:
                    oproj((2 * ga + 1) // 8)

        gd["qkv"] = [p.last("tensor")]
        gd["ofm"] = [p.last("tensor")]

    if stage in ("ln", "ln0"):
        dump_and_finish("xn")
    if stage in ("a1", "a2", "a3", "a4"):
        emit_atile(0, {"a1": "qkv", "a2": "sm", "a3": "tp", "a4": "av"}[stage])
        base = p.barrier_tokens()
        dumps = []
        if stage == "a1":
            for ic in range(8):
                dumps.append((y_h[ic * 128:(ic + 1) * 128, 0:512], q_fm[ic][:]))
                dumps.append((y_h[ic * 128:(ic + 1) * 128, 512:1024],
                              k_fm[ic][:, 0:512]))
            for tb in range(5):
                dumps.append((y_h[1024 + tb * 128:1152 + tb * 128, :],
                              v_tok[:, tb * H:(tb + 1) * H]))
        elif stage == "a2":
            dumps.append((y_h[0:128, 0:268], s_m[0][:]))
            dumps.append((y_h[128:256, 0:268], p_sb[0][:]))
            dumps.append((y_h[256:384, 0:268], p_n[0][:]))
            dumps.append((y_h[384:512, 0:2], nm_sb[0][:]))
            dumps.append((y_h[512:640, 0:2], z_sb[0][:]))
            dumps.append((y_h[640:768, 0:2], rz_sb[0][:]))
            dumps.append((y_h[768:896, 0:268], s_m[1][:]))
            dumps.append((y_h[896:1024, 0:268], p_n[1][:]))
        elif stage == "a3":
            dumps.append((y_h[0:128, 0:512], at1[0][:]))
            dumps.append((y_h[128:134, 0:512], at2[0][0:6, :]))
        elif stage == "a4":
            dumps.append((y_h[0:128, 0:512], o_fm[0][:]))
            dumps.append((y_h[128:256, 0:512], o_fm[1][:]))
        tprev = None
        for dst, srcap in dumps:
            tprev = E("gpsimd", lambda e, d=dst, s=srcap: e.dma_start(d, s),
                      waits=list(base) + ([tprev] if tprev else []),
                      stream="dY0")
        t_last_y[0] = tprev
    if stage in ("full", "attn", "b1", "b2g"):
        for a in range(NA):
            emit_atile(a)
    if stage == "attn":
        dump_and_finish("acc")

    t_phA = p.barrier_tokens()

    g_wg = [[], []]
    g_wu = [[], []]
    hm_prev = [None, None, None, None]
    nhc = (HC if stage in ("full", "b1", "b2g")
           else (1 if stage == "b1g" else 0))
    if nhc:
        E("scalar",
          lambda e: e.add_instruction(mybir.InstLoadActFuncSet(
              name=nc.get_next_instruction_name(),
              act_func_set_id=18, ins=[], outs=[])),
          waits=t_phA)
    for hc in range(nhc):
        par = hc % 2
        t_wg = E("gpsimd",
                 lambda e, c=hc, d=wgst[par]:
                     e.dma_start(d[:], wg_hh[c * 128:(c + 1) * 128, :]),
                 waits=list(g_wg[par]) + (t_phA if hc == 0 else []),
                 stream=f"dG{par}")
        t_wu = E("gpsimd",
                 lambda e, c=hc, d=wust[par]:
                     e.dma_start(d[:], wu_hh[c * 128:(c + 1) * 128, :]),
                 waits=list(g_wu[par]), stream=f"dU{par}")
        gate_last = []
        for ic in range(8):
            for g in range(4):
                bi = g
                tk = E("tensor",
                       lambda e, i=ic, gg=g, t=pbf[bi], w=wgst[par]:
                           e.matmul(t[:], w[:, i * 128:(i + 1) * 128],
                                    xn[i][:, 3 + gg * 512:3 + gg * 512 + 512],
                                    start=(i == 0), stop=(i == 7),
                                    skip_group_check=True),
                       waits=(([bank_free[bi]] if ic == 0 else []) +
                              ([t_wg] + (t_phA if hc == 0 else [])
                               if (ic == 0 and g == 0) else [])))
                if ic == 7:
                    bank_done[bi] = tk
                    gate_last.append(tk)
        g_wg[par] = [gate_last[-1]]
        t_silu = []
        for g in range(4):
            bi = g
            ts_ = E("scalar",
                    lambda e, gg=g, t=pbf[bi]:
                        e.activation(htmp[:, gg * 512:gg * 512 + 512], t[:],
                                     AF.Silu, scale=QS),
                    waits=[bank_done[bi], hm_prev[g]] +
                          (t_phA if hc == 0 and g == 0 else []))
            bank_free[bi] = ts_
            t_silu.append(ts_)
        up_last = []
        for ic in range(8):
            for g in range(4):
                bi = 4 + g
                tk = E("tensor",
                       lambda e, i=ic, gg=g, t=pbf[bi], w=wust[par]:
                           e.matmul(t[:], w[:, i * 128:(i + 1) * 128],
                                    xn[i][:, 3 + gg * 512:3 + gg * 512 + 512],
                                    start=(i == 0), stop=(i == 7),
                                    skip_group_check=True),
                       waits=(([bank_free[bi]] if ic == 0 else []) +
                              ([t_wu] if (ic == 0 and g == 0) else [])))
                if ic == 7:
                    bank_done[bi] = tk
                    up_last.append(tk)
        g_wu[par] = [up_last[-1]]
        for g in range(4):
            bi = 4 + g
            bank_free[bi] = E("vector",
                              lambda e, c=hc, gg=g, t=pbf[bi]:
                                  e.tensor_tensor(
                                      h_sb[c][:, gg * 512:gg * 512 + 512],
                                      htmp[:, gg * 512:gg * 512 + 512],
                                      t[:], ALU.mult),
                              waits=[bank_done[bi], t_silu[g]] +
                                    (t_phA if hc == 0 and g == 0 else []))
            hm_prev[g] = bank_free[bi]

    if stage == "b1":
        dump_and_finish("h")
    if stage == "b1g":
        base = p.barrier_tokens()
        dumps = [(0, wgst[0][:, 0:1024]),
                 (1, htmp[:, 0:1024]),
                 (2, h_sb[0][:, 0:1024]),
                 (3, xn[0][:, 0:1024]),
                 (4, pbf[0][:, 0:512]),
                 (5, pbf[4][:, 0:512])]
        tprev = None
        for row, srcap in dumps:
            tv = E("vector",
                   lambda e, s=srcap, n=srcap.shape[-1]:
                       e.tensor_copy(y_sb[0][:, 0:n], s),
                   waits=list(base) + ([tprev] if tprev else []))
            tprev = E("sync",
                      lambda e, rr=row: e.dma_start(
                          y_h[rr * 128:(rr + 1) * 128, :], y_sb[0][:]),
                      waits=[tv], stream="dY0")
        t_last_y[0] = tprev

    t_phB = p.barrier_tokens()

    t_wd = []
    for hc in range(HC if stage in ("full", "b2g") else 0):
        t_wd.append(E("gpsimd",
                      lambda e, c=hc:
                          e.dma_start(wdT[c][:],
                                      wd_hh[c * 128:(c + 1) * 128, :]),
                      waits=(t_phB if hc == 0 else []),
                      stream="dD"))

    t_ystore = [None, None]
    for tb in range(NQB if stage == "full" else (1 if stage == "b2g" else 0)):
        evs = []
        for half in range(2):
            bi = (tb % 2) * 2 + half
            tk = None
            for hc in range(HC):
                w = []
                if hc == 0:
                    w = [bank_free[bi], t_wd[-1]] + (t_phB
                                                     if tb == 0 and half == 0
                                                     else [])
                tk = E("tensor",
                       lambda e, c=hc, tt=tb, hh=half, t=pbf[bi]:
                           e.matmul(t[:], h_sb[c][:, tt * 128:tt * 128 + 128],
                                    wdT[c][:, hh * 512:(hh + 1) * 512],
                                    start=(c == 0), stop=(c == HC - 1),
                                    skip_group_check=True),
                       waits=w)
            bank_done[bi] = tk
            tv = E("vector",
                   lambda e, tt=tb, hh=half, t=pbf[bi], d=y_sb[tb % 2]:
                       e.scalar_tensor_tensor(
                           d[:, hh * 512:hh * 512 + 512], t[:], QS2,
                           acc[:, tt * H + hh * 512:tt * H + hh * 512 + 512],
                           ALU.mult, ALU.add),
                   waits=[bank_done[bi]] +
                         ([t_ystore[tb % 2]] if t_ystore[tb % 2] else []))
            bank_free[bi] = tv
            evs.append(tv)
        t_ystore[tb % 2] = E("sync",
                             lambda e, tt=tb, s=y_sb[tb % 2]:
                                 e.dma_start(y_h[tt * 128:tt * 128 + 128, :],
                                             s[:]),
                             waits=evs, stream=f"dY{tb % 2}")
        if tb == NQB - 1 or stage == "b2g":
            t_last_y[0] = t_ystore[tb % 2]
    if stage == "b2g":
        base = p.barrier_tokens()
        tprev = None
        for row, srcap in ((1, wdT[0][:, 0:1024]), (2, wdT[23][:, 0:1024]),
                           (3, acc[:, 0:1024]), (4, h_sb[0][:, 0:1024])):
            tv = E("vector",
                   lambda e, s=srcap: e.tensor_copy(y_sb[1][:], s),
                   waits=list(base) + ([tprev] if tprev else []))
            tprev = E("sync",
                      lambda e, rr=row: e.dma_start(
                          y_h[rr * 128:(rr + 1) * 128, :], y_sb[1][:]),
                      waits=[tv], stream="dY1")
        t_last_y[0] = tprev

    final_tok = t_last_y[0]
    all_keys = set(p.nkey)
    dma_keys = all_keys - set(("tensor", "vector", "scalar"))
    waited = {k: set() for k in all_keys}
    for k in all_keys:
        if k not in dma_keys:
            waited[k] = set(range(p.nkey[k]))
    for e in ENGINES:
        for _, ws, _k in p.ops[e]:
            for (wk_, wi) in ws:
                waited[wk_].add(wi)
    waited[final_tok[0]].add(final_tok[1])
    rank = {}
    for k in all_keys:
        if k in dma_keys:
            rank[k] = {i: i + 1 for i in range(p.nkey[k])}
        else:
            rank[k] = {i: r + 1 for r, i in enumerate(sorted(waited[k]))}
    unit = {k: (16 if k in dma_keys else 1) for k in all_keys}

    import contextlib
    with contextlib.ExitStack() as stk:
        semh = {k: stk.enter_context(nc.semaphore("c_" + str(k)))
                for k in all_keys}
        kcnt = {k: 0 for k in all_keys}
        with nc.Block() as block:
            def make(eng_name):
                def run(e):
                    if eng_name == "scalar":
                        e.add_instruction(mybir.InstLoadActFuncSet(
                            name=nc.get_next_instruction_name(),
                            act_func_set_id=6, ins=[], outs=[]))
                    for fn, ws, key in p.ops[eng_name]:
                        for (wk_, wi) in ws:
                            e.wait_ge(semh[wk_], rank[wk_][wi] * unit[wk_])
                        inst = fn(e)
                        i = kcnt[key]
                        kcnt[key] += 1
                        if key in dma_keys or i in waited[key]:
                            inst.then_inc(semh[key], unit[key])
                    if eng_name == "sync":
                        e.wait_ge(semh[final_tok[0]],
                                  rank[final_tok[0]][final_tok[1]] *
                                  unit[final_tok[0]])
                return run

            block.sync(make("sync"))
            block.gpsimd(make("gpsimd"))
            block.tensor(make("tensor"))
            block.vector(make("vector"))
            block.scalar(make("scalar"))
    return nc


_NC = None


def _get_nc():
    global _NC
    if _NC is None:
        _NC = build_nc()
    return _NC


def _band_mask(first, last):
    m = np.full((128, 134), NEG, np.float32)
    for r in range(128):
        lo = max(r, 3) if first else r
        hi = min(r + 6, 130) if last else r + 6
        m[r, lo:hi + 1] = 0.0
    return np.concatenate([m, m], axis=1).astype(ml_dtypes.bfloat16)


def kernel(L0_attn_hidden_states, L0_attn_w_q, L0_attn_b_q, L0_attn_w_k,
           L0_attn_b_k, L0_attn_w_v, L0_attn_b_v, L0_attn_w_o, L0_attn_b_o,
           L0_ffn_w_gate, L0_ffn_w_up, L0_ffn_w_down):
    x = np.ascontiguousarray(np.asarray(L0_attn_hidden_states, np.float32))
    wq = np.ascontiguousarray(np.asarray(L0_attn_w_q, np.float32))
    wk = np.ascontiguousarray(np.asarray(L0_attn_w_k, np.float32))
    wv = np.ascontiguousarray(np.asarray(L0_attn_w_v, np.float32))
    wo = np.ascontiguousarray(np.asarray(L0_attn_w_o, np.float32))
    bq = np.asarray(L0_attn_b_q, np.float32)
    bk = np.asarray(L0_attn_b_k, np.float32)
    bv = np.asarray(L0_attn_b_v, np.float32)
    bo = np.asarray(L0_attn_b_o, np.float32)
    wg = np.asarray(L0_ffn_w_gate, np.int32)
    wu = np.asarray(L0_ffn_w_up, np.int32)
    wd = np.asarray(L0_ffn_w_down, np.int32)

    wg_b = np.ascontiguousarray(
        wg.reshape(HC, 128, 8, 128).transpose(0, 3, 2, 1).reshape(FH, H))
    wu_b = np.ascontiguousarray(
        wu.reshape(HC, 128, 8, 128).transpose(0, 3, 2, 1).reshape(FH, H))
    wd_b = np.ascontiguousarray(wd.T)

    shared = dict(
        wq=wq, wk=wk, wv=wv, wo=wo,
        bq=np.ascontiguousarray(bq.reshape(8, 128).T),
        bk=np.ascontiguousarray(bk.reshape(8, 128).T),
        bv=np.ascontiguousarray(bv.reshape(8, 128).T),
        bo_row=np.ascontiguousarray(bo.reshape(1, H)),
        wg_b=wg_b, wu_b=wu_b, wd_b=wd_b,
        ident=np.eye(128, dtype=ml_dtypes.bfloat16),
        ident_f=np.eye(128, dtype=np.float32),
        ones_row=np.ones((1, 128), dtype=ml_dtypes.bfloat16),
        eps_col=np.full((128, 1), EPS, np.float32),
    )
    m_mid = _band_mask(False, False)
    m_first = _band_mask(True, False)
    m_last = _band_mask(False, True)

    in_maps = []
    for core in range(NCORES):
        b_i, qi = core // 4, core % 4
        t0 = qi * TOK
        x_ext = np.zeros((EXT, H), np.float32)
        lo = t0 - 3
        s0, s1 = max(0, lo), min(S, lo + EXT)
        x_ext[s0 - lo:s1 - lo] = x[b_i, s0:s1]
        im = dict(shared)
        im["x_tok"] = x_ext
        im["mask_first"] = m_first if qi == 0 else m_mid
        im["mask_mid"] = m_mid
        im["mask_last"] = m_last if qi == 3 else m_mid
        in_maps.append(im)

    res = run_bass_kernel_spmd(_get_nc(), in_maps, core_ids=list(range(NCORES)))
    y = np.empty((B, S, H), np.float32)
    for core in range(NCORES):
        b_i, qi = core // 4, core % 4
        y[b_i, qi * TOK:(qi + 1) * TOK] = res.results[core]["y"]
    return y


if __name__ == "__main__":
    np.random.seed(0)
    import reference as ref
    ins = {k: np.asarray(v) for k, v in ref.setup_inputs().items()}
    out = kernel(**ins)
    exp = np.asarray(ref.reference(**ins))
    print("rvr:", float(np.mean((out - exp) ** 2) / np.mean(exp ** 2)))
    print("maxabs:", float(np.abs(out - exp).max()))


# version 15
# speedup vs baseline: 1.0227x; 1.0030x over previous
import numpy as np
import ml_dtypes
import concourse.bass as bass
import concourse.mybir as mybir
from concourse.bass_utils import run_bass_kernel_spmd

F32 = mybir.dt.float32
BF16 = mybir.dt.bfloat16
I32 = mybir.dt.int32
AX = mybir.AxisListType.X
AF = mybir.ActivationFunctionType
ALU = mybir.AluOpType

B, S, H = 2, 8192, 1024
NCORES = 8
TOK = 2048
EXT = 2176
NTILE = 17
SCALE = 0.125
FH, HC = 3072, 24
QS = 1.0 / 127.0
QS2 = QS * QS
NA = 4
NQB = 16
EPS = 1e-5
NEG = -1e30

ENGINES = ("sync", "gpsimd", "tensor", "vector", "scalar")
DMA_ENG = {"sync": True, "gpsimd": True, "tensor": False, "vector": False,
           "scalar": False}


class Prog:

    def __init__(self):
        self.ops = {e: [] for e in ENGINES}
        self.nkey = {}
        self.last_of = {}

    def emit(self, eng, fn, waits=(), stream=None):
        key = stream if stream is not None else eng
        idx = self.nkey.get(key, 0)
        self.nkey[key] = idx + 1
        tok = (key, idx)
        self.ops[eng].append((fn, [w for w in waits if w is not None], key))
        self.last_of[key] = tok
        return tok

    def last(self, eng):
        return self.last_of.get(eng)

    def barrier_tokens(self):
        return list(self.last_of.values())


def build_nc(stage="full"):
    nc = bass.Bass("TRN2", target_bir_lowering=False)
    p = Prog()

    x_tok_h = nc.dram_tensor("x_tok", [EXT, H], F32, kind="ExternalInput").ap()
    wq_h = nc.dram_tensor("wq", [H, H], F32, kind="ExternalInput").ap()
    wk_h = nc.dram_tensor("wk", [H, H], F32, kind="ExternalInput").ap()
    wv_h = nc.dram_tensor("wv", [H, H], F32, kind="ExternalInput").ap()
    wo_h = nc.dram_tensor("wo", [H, H], F32, kind="ExternalInput").ap()
    bq_h = nc.dram_tensor("bq", [128, 8], F32, kind="ExternalInput").ap()
    bk_h = nc.dram_tensor("bk", [128, 8], F32, kind="ExternalInput").ap()
    bv_h = nc.dram_tensor("bv", [128, 8], F32, kind="ExternalInput").ap()
    bo_h = nc.dram_tensor("bo_row", [1, H], F32, kind="ExternalInput").ap()
    wg_hh = nc.dram_tensor("wg_b", [FH, H], I32, kind="ExternalInput").ap()
    wu_hh = nc.dram_tensor("wu_b", [FH, H], I32, kind="ExternalInput").ap()
    wd_hh = nc.dram_tensor("wd_b", [FH, H], I32, kind="ExternalInput").ap()
    mskf_h = nc.dram_tensor("mask_first", [128, 268], BF16, kind="ExternalInput").ap()
    mskm_h = nc.dram_tensor("mask_mid", [128, 268], BF16, kind="ExternalInput").ap()
    mskl_h = nc.dram_tensor("mask_last", [128, 268], BF16, kind="ExternalInput").ap()
    idn_h = nc.dram_tensor("ident", [128, 128], BF16, kind="ExternalInput").ap()
    idf_h = nc.dram_tensor("ident_f", [128, 128], F32, kind="ExternalInput").ap()
    ones_h = nc.dram_tensor("ones_row", [1, 128], BF16, kind="ExternalInput").ap()
    eps_h = nc.dram_tensor("eps_col", [128, 1], F32, kind="ExternalInput").ap()
    y_h = nc.dram_tensor("y", [TOK, H], F32, kind="ExternalOutput").ap()

    SBUF_BASE = 16640
    SBUF_LIMIT = 229376
    _off = [SBUF_BASE]

    def at(name, shape, dtype, off=None):
        nb = shape[-1] * mybir.dt.size(dtype)
        if off is None:
            off = _off[0]
            _off[0] = off + ((nb + 255) // 256) * 256
            assert _off[0] <= SBUF_LIMIT, (name, _off[0])
        return nc.alloc_sbuf_tensor_at(name, list(shape), dtype, offset=off)

    mskf = at("mskf", [128, 268], BF16)
    mskm = at("mskm", [128, 268], BF16)
    mskl = at("mskl", [128, 268], BF16)
    bq_sb = at("bq_sb", [128, 8], F32)
    bk_sb = at("bk_sb", [128, 8], F32)
    bv_sb = at("bv_sb", [128, 8], BF16)
    bqs_sb = at("bqs_sb", [128, 8], F32)
    bo_sb = at("bo_sb", [1, H], F32)
    beff_sb = at("beff_sb", [1, H], BF16)
    idn_sb = at("idn_sb", [128, 128], BF16)
    idf_sb = at("idf_sb", [128, 128], F32)
    ones_sb = at("ones_sb", [1, 128], BF16)
    eps_sb = at("eps_sb", [128, 1], F32)
    st = {k: at("st_" + k, [128, NTILE], F32)
          for k in ("s1", "s2", "ms", "m2", "va", "ln", "a", "b")}
    acc = at("acc", [128, NQB * H], BF16)
    y_sb = [at(f"y_sb{i}", [128, H], F32) for i in range(2)]

    xn0 = _off[0]
    xn = [at(f"xn{ic}", [128, EXT], BF16) for ic in range(8)]
    h0 = _off[0]
    h_sb = [at(f"h{c}", [128, TOK], BF16) for c in range(HC)]
    top0 = _off[0]
    wgst = [at(f"wgst{i}", [128, H], BF16) for i in range(2)]
    wust = [at(f"wust{i}", [128, H], BF16) for i in range(2)]
    htmp = at("htmp", [128, TOK], BF16)
    assert _off[0] <= SBUF_LIMIT

    wdT = []
    for c in range(HC):
        off = xn0 + 2048 * c if c < 17 else top0 + 2048 * (c - 17)
        wdT.append(at(f"wdT{c}", [128, H], BF16, off=off))

    a_off = [h0]

    def atA(name, shape, dtype):
        nb = shape[-1] * mybir.dt.size(dtype)
        off = a_off[0]
        a_off[0] = off + ((nb + 255) // 256) * 256
        assert a_off[0] <= SBUF_LIMIT, (name, a_off[0])
        return nc.alloc_sbuf_tensor_at(name, list(shape), dtype, offset=off)

    wq_sb = [atA(f"wqb{i}", [128, H], BF16) for i in range(8)]
    wk_sb = [atA(f"wkb{i}", [128, H], BF16) for i in range(8)]
    wv_sb = [atA(f"wvb{i}", [128, H], BF16) for i in range(8)]
    wo_sb = [atA(f"wob{i}", [128, H], BF16) for i in range(8)]
    ln0 = a_off[0]
    q_fm = [atA(f"qfm{i}", [128, 512], BF16) for i in range(8)]
    k_fm = [atA(f"kfm{i}", [128, 528], BF16) for i in range(8)]
    v_tok = atA("v_tok", [128, 5 * H], BF16)
    o_fm = [atA(f"ofm{i}", [128, 512], BF16) for i in range(8)]
    p_sb = [atA(f"p_sb{i}", [128, 268], BF16) for i in range(2)]
    p_n = [atA(f"p_n{i}", [128, 268], F32) for i in range(4)]
    nm_sb = [atA(f"nm{i}", [128, 2], F32) for i in range(2)]
    z_sb = [atA(f"z{i}", [128, 2], F32) for i in range(2)]
    rz_sb = [atA(f"rz{i}", [128, 2], F32) for i in range(2)]
    at1 = [atA(f"at1_{i}", [128, 256], BF16) for i in range(4)]
    at2 = [atA(f"at2_{i}", [6, 256], BF16) for i in range(4)]
    x_res = [atA(f"xres{i}", [128, H], F32) for i in range(2)]
    lnoff = [ln0]

    def atLN(name, shape, dtype):
        nb = shape[-1] * mybir.dt.size(dtype)
        off = lnoff[0]
        lnoff[0] = off + ((nb + 255) // 256) * 256
        return nc.alloc_sbuf_tensor_at(name, list(shape), dtype, offset=off)

    xst = [atLN(f"xst{i}", [128, H], F32) for i in range(2)]
    xnst = [atLN(f"xnst{i}", [128, H], F32) for i in range(2)]
    sq_scr = atLN("sq_scr", [128, H], BF16)

    pb = [nc.alloc_psum_tensor(f"pb{i}", [128, 512], F32) for i in range(8)]
    pbf = [t.ap() for t in pb]

    E = p.emit

    t_consts = []
    for dst, src in ((mskf, mskf_h), (mskm, mskm_h), (mskl, mskl_h),
                     (bq_sb, bq_h), (bk_sb, bk_h),
                     (bo_sb, bo_h), (idn_sb, idn_h), (idf_sb, idf_h),
                     (ones_sb, ones_h),
                     (eps_sb, eps_h)):
        t_consts.append(E("sync", lambda e, d=dst, s=src: e.dma_start(d[:], s[:]),
                          stream="dC"))
    t_cAll = t_consts[-1]
    t_bv = E("gpsimd", lambda e: e.dma_start(bv_sb[:], bv_h[:]), stream="dBv")

    t_w = {}
    wlist = (() if stage == "ln0" else
             (("q", wq_sb, wq_h), ("k", wk_sb, wk_h),
              ("v", wv_sb, wv_h), ("o", wo_sb, wo_h)))
    for nm, bufs, src in wlist:
        t = None
        for ic in range(8):
            t = E("gpsimd",
                  lambda e, d=bufs[ic], s=src, i=ic:
                      e.dma_start(d[:], s[i * 128:(i + 1) * 128, :]),
                  stream="dW" + nm)
        t_w[nm] = t

    t_bqs = E("vector", lambda e: e.tensor_scalar_mul(bqs_sb[:], bq_sb[:], SCALE),
              waits=[t_cAll])

    t_beff = []
    bank_free = [None] * 8
    bank_done = [None] * 8
    for half in range(2 if stage != "ln0" else 0):
        tk = None
        for ic in range(8):
            tk = E("tensor",
                   lambda e, i=ic, t=pbf[half], hh=half:
                       e.matmul(t[0:1, :], bv_sb[:, i:i + 1],
                                wo_sb[i][:, hh * 512:(hh + 1) * 512],
                                start=(i == 0), stop=(i == 7),
                                skip_group_check=True),
                   waits=([t_w["o"], t_bv] if ic == 0 else []))
        bank_done[half] = tk
        tv = E("vector",
               lambda e, t=pbf[half], hh=half:
                   e.tensor_tensor(beff_sb[0:1, hh * 512:(hh + 1) * 512],
                                   t[0:1, :], bo_sb[0:1, hh * 512:(hh + 1) * 512],
                                   ALU.add),
               waits=[bank_done[half], t_cAll])
        bank_free[half] = tv
        t_beff.append(tv)

    inv_h = 1.0 / H
    xst_guard = [[], []]
    xnst_guard = [[], []]
    ln_bank_guard = [[], []]
    ln_tile_tok = []
    for i in range(NTILE):
        par = i % 2
        t_x = E("sync",
                lambda e, j=i, d=xst[par]:
                    e.dma_start(d[:], x_tok_h[j * 128:(j + 1) * 128, :]),
                waits=xst_guard[par], stream=f"dX{par}")
        t_s1 = E("vector",
                 lambda e, j=i, s=xst[par]:
                     e.reduce_sum(st["s1"][:, j:j + 1], s[:], axis=AX),
                 waits=[t_x])
        t_sq = E("scalar",
                 lambda e, j=i, s=xst[par]:
                     e.activation(sq_scr[:], s[:], AF.Square,
                                  accum_out=st["s2"][:, j:j + 1]),
                 waits=[t_x])
        t_ms = E("vector",
                 lambda e, j=i:
                     e.tensor_scalar_mul(st["ms"][:, j:j + 1],
                                         st["s1"][:, j:j + 1], inv_h),
                 waits=[t_s1])
        t_m2 = E("vector",
                 lambda e, j=i:
                     e.tensor_mul(st["m2"][:, j:j + 1], st["ms"][:, j:j + 1],
                                  st["ms"][:, j:j + 1]),
                 waits=[t_ms])
        t_va = E("vector",
                 lambda e, j=i:
                     e.scalar_tensor_tensor(st["va"][:, j:j + 1],
                                            st["s2"][:, j:j + 1], inv_h,
                                            st["m2"][:, j:j + 1],
                                            ALU.mult, ALU.subtract),
                 waits=[t_sq, t_m2])
        t_ln = E("scalar",
                 lambda e, j=i:
                     e.activation(st["ln"][:, j:j + 1], st["va"][:, j:j + 1],
                                  AF.Ln, bias=eps_sb[:, 0:1]),
                 waits=[t_va, t_cAll])
        t_a = E("scalar",
                lambda e, j=i:
                    e.activation(st["a"][:, j:j + 1], st["ln"][:, j:j + 1],
                                 AF.Exp, scale=-0.5),
                waits=[t_ln])
        t_b = E("vector",
                lambda e, j=i:
                    e.tensor_scalar(st["b"][:, j:j + 1], st["ms"][:, j:j + 1],
                                    st["a"][:, j:j + 1], -1.0,
                                    ALU.mult, ALU.mult),
                waits=[t_a])
        t_ap = E("scalar",
                 lambda e, j=i, s=xst[par], d=xnst[par]:
                     e.activation(d[:], s[:], AF.Identity,
                                  bias=st["b"][:, j:j + 1],
                                  scale=st["a"][:, j:j + 1]),
                 waits=[t_b] + xnst_guard[par])
        xst_guard[par] = [t_sq, t_ap]
        t_tr = None
        for ic in range(8):
            t_tr = E("tensor",
                     lambda e, j=i, c=ic, s=xnst[par], bk=pbf[6 + (ic // 4)]:
                         e.transpose(bk[:, (c % 4) * 128:(c % 4) * 128 + 128],
                                     s[:, c * 128:(c + 1) * 128], idf_sb[:]),
                     waits=([t_ap, t_cAll] + ln_bank_guard[0]
                            if ic == 0 else []))
        xnst_guard[par] = [t_tr]
        evs = []
        for ic in range(8):
            eng = "vector" if ic < 4 else "scalar"
            fn = ((lambda e, j=i, c=ic, bk=pbf[6]:
                       e.tensor_copy(xn[c][:, j * 128:(j + 1) * 128],
                                     bk[:, (c % 4) * 128:(c % 4) * 128 + 128]))
                  if eng == "vector" else
                  (lambda e, j=i, c=ic, bk=pbf[7]:
                       e.activation(xn[c][:, j * 128:(j + 1) * 128],
                                    bk[:, (c % 4) * 128:(c % 4) * 128 + 128],
                                    AF.Identity)))
            evs.append(E(eng, fn, waits=[t_tr] if ic in (0, 4) else []))
        ln_bank_guard[0] = [evs[3], evs[7]]
        ln_tile_tok.append([evs[3], evs[7]])
    t_ln_all = p.barrier_tokens()

    gd = {"exps": {}, "pn": {}, "tp": {}, "xres": [[], []],
          "qkv": [], "ofm": [], "first_xres": True,
          "avm": {}, "avmp": {}, "atw": [[], []], "att": [None, None],
          "xr": {}}
    t_last_y = [None]

    def dump_and_finish(kind):
        tprev = None
        base = p.barrier_tokens()
        if kind == "xn":
            for ki, kk in enumerate(("s1", "s2", "ms", "m2", "va", "ln", "a", "b")):
                tprev = E("sync",
                          lambda e, k=ki, t=st[kk]:
                              e.dma_start(y_h[1024 + 128 * k:1152 + 128 * k, 0:NTILE],
                                          t[:]),
                          waits=list(base) + ([tprev] if tprev else []),
                          stream="dY0")
        n = 8 if kind == "xn" else NQB
        for i in range(n):
            w = list(base) + ([tprev] if tprev else [])
            if kind == "xn":
                tv = E("vector",
                       lambda e, c=i, d=y_sb[i % 2]:
                           e.tensor_copy(d[:], xn[c][:, 0:H]), waits=w)
            elif kind == "acc":
                tv = E("vector",
                       lambda e, t=i, d=y_sb[i % 2]:
                           e.tensor_copy(d[:], acc[:, t * H:(t + 1) * H]),
                       waits=w)
            else:
                tv = E("vector",
                       lambda e, c=i, d=y_sb[i % 2]:
                           e.tensor_copy(d[:], h_sb[c][:, 0:H]), waits=w)
            tprev = E("sync",
                      lambda e, t=i, s=y_sb[i % 2]:
                          e.dma_start(y_h[t * 128:(t + 1) * 128, :], s[:]),
                      waits=[tv], stream=f"dY{i % 2}")
        t_last_y[0] = tprev

    def emit_atile(a, upto="full"):
        ext0 = 512 * a
        ln_wait = list(ln_tile_tok[min(4 * a + 4, NTILE - 1)])

        for jc in range(8):
            bi = jc % 2
            tk = None
            for ic in range(8):
                tk = E("tensor",
                       lambda e, i=ic, j=jc, t=pbf[bi], e0=ext0:
                           e.matmul(t[:], wq_sb[i][:, j * 128:(j + 1) * 128],
                                    xn[i][:, e0 + 3:e0 + 515],
                                    start=(i == 0), stop=(i == 7),
                                    skip_group_check=True),
                       waits=([bank_free[bi], t_w["q"], t_bqs] + ln_wait
                              if ic == 0 else []))
            bank_done[bi] = tk
            bank_free[bi] = E("scalar",
                              lambda e, j=jc, t=pbf[bi]:
                                  e.activation(q_fm[j][:], t[:], AF.Identity,
                                               bias=bqs_sb[:, j:j + 1],
                                               scale=SCALE),
                              waits=[bank_done[bi]] + list(gd["qkv"]))
        t_qev = [bank_free[0], bank_free[1]]
        for jc in range(8):
            for grp, (c0, cw) in enumerate(((0, 512), (512, 6))):
                bi = grp
                tk = None
                for ic in range(8):
                    tk = E("tensor",
                           lambda e, i=ic, j=jc, t=pbf[bi], e0=ext0,
                                  cc=c0, ww=cw:
                               e.matmul(t[:, 0:ww],
                                        wk_sb[i][:, j * 128:(j + 1) * 128],
                                        xn[i][:, e0 + cc:e0 + cc + ww],
                                        start=(i == 0), stop=(i == 7),
                                        skip_group_check=True),
                           waits=([bank_free[bi], t_w["k"], t_cAll]
                                  + ln_wait if ic == 0 else []))
                bank_done[bi] = tk
                bank_free[bi] = E("scalar",
                                  lambda e, j=jc, t=pbf[bi], cc=c0, ww=cw:
                                      e.activation(k_fm[j][:, cc:cc + ww],
                                                   t[:, 0:ww], AF.Identity,
                                                   bias=bk_sb[:, j:j + 1]),
                                  waits=[bank_done[bi]] + list(gd["qkv"]))
        t_kev = [bank_free[0], bank_free[1]]
        for tb in range(5):
            for half in range(2):
                bi = half
                tk = None
                for ic in range(8):
                    tk = E("tensor",
                           lambda e, i=ic, t=pbf[bi], m=4 * a + tb, hh=half:
                               e.matmul(t[:],
                                        xn[i][:, m * 128:m * 128 + 128],
                                        wv_sb[i][:, hh * 512:(hh + 1) * 512],
                                        start=(i == 0), stop=(i == 7),
                                        skip_group_check=True),
                           waits=([bank_free[bi], t_w["v"]] + ln_wait
                                  if ic == 0 else []))
                bank_done[bi] = tk
                bank_free[bi] = E("scalar",
                                  lambda e, t=pbf[bi], s=tb, hh=half:
                                      e.activation(
                                          v_tok[:, s * H + hh * 512:
                                                s * H + hh * 512 + 512],
                                          t[:], AF.Identity),
                                  waits=[bank_done[bi]] + list(gd["qkv"]))
        t_vev = [bank_free[0], bank_free[1]]
        t_qkv = t_qev + t_kev + t_vev
        if upto == "qkv":
            return

        def msel(jl):
            g = 4 * a + jl
            return mskf if g == 0 else (mskl if g == NQB - 1 else mskm)

        def sc_pair(pr):
            gp = 32 * a + pr
            jl, hp = pr // 8, pr % 8
            m = msel(pr // 8)
            toks = []
            for s in range(2):
                bi = (2 if s == 0 else 4) + gp % 2
                hd = hp * 2 + s
                jc, off = hd // 2, (hd % 2) * 64
                toks.append(E("tensor",
                              lambda e, t=pbf[bi], jj=jl, c=jc, o=off:
                                  e.matmul(t[:, 0:134],
                                           q_fm[c][o:o + 64,
                                                   jj * 128:jj * 128 + 128],
                                           k_fm[c][o:o + 64,
                                                   jj * 128:jj * 128 + 134],
                                           start=True, stop=False,
                                           skip_group_check=True),
                              waits=[bank_free[bi]] + (t_qkv if s == 0 else [])))
                toks.append(E("tensor",
                              lambda e, t=pbf[bi], mm=m, ss=s:
                                  e.matmul(t[:, 0:134], idn_sb[:],
                                           mm[:, ss * 134:ss * 134 + 134],
                                           start=False, stop=True,
                                           skip_group_check=True),
                              waits=[t_cAll] if gp < 10 else []))
                bank_done[bi] = toks[-1]
            gd[("scmm", gp)] = toks

        def sm_pair(pr):
            gp = 32 * a + pr
            par = gp % 2
            t_exps = []
            for s in range(2):
                sbi = (2 if s == 0 else 4) + gp % 2
                t_nm = E("vector",
                         lambda e, t=pbf[sbi], d=nm_sb[par], ss=s:
                             e.reduce_max(d[:, ss:ss + 1], t[:, 0:134],
                                          axis=AX, negate=True),
                         waits=[bank_done[sbi]] +
                               (gd["exps"].get(gp - 2, []) if s == 0 else []))
                t_exps.append(E("scalar",
                                lambda e, t=pbf[sbi], ss=s, d=p_sb[par],
                                       nn=nm_sb[par], zz=z_sb[par]:
                                    e.activation(d[:, ss * 134:ss * 134 + 134],
                                                 t[:, 0:134],
                                                 AF.Exp, bias=nn[:, ss:ss + 1],
                                                 accum_out=zz[:, ss:ss + 1]),
                                waits=[t_nm] +
                                      (gd["pn"].get(gp - 2, [])
                                       if s == 0 else [])))
                bank_free[sbi] = t_exps[-1]
            gd["exps"][gp] = t_exps
            t_rz = E("vector",
                     lambda e, zz=z_sb[par], rr=rz_sb[par]:
                         e.reciprocal(rr[:], zz[:]),
                     waits=list(t_exps))
            t_pn = []
            for s in range(2):
                t_pn.append(E("vector",
                              lambda e, ss=s, src=p_sb[par], d=p_n[gp % 4],
                                     rr=rz_sb[par]:
                                  e.tensor_scalar_mul(
                                      d[:, ss * 134:ss * 134 + 134],
                                      src[:, ss * 134:ss * 134 + 134],
                                      rr[:, ss:ss + 1]),
                              waits=([t_rz] + gd["tp"].get(gp - 4, [])
                                     if s == 0 else [])))
            gd["pn"][gp] = t_pn

        def tp_group(gl):
            tlast = None
            for i in (0, 1):
                gp = 32 * a + 2 * gl + i
                tbk = 6 if gp % 2 == 0 else 1
                first = [gd["pn"][gp][-1]] + gd["atw"][gp % 2]
                if gp == 0:
                    first.append(t_cAll)
                for s in range(2):
                    E("tensor",
                      lambda e, src=p_n[gp % 4], ss=s, bk=pbf[tbk]:
                          e.transpose(bk[:, ss * 128:ss * 128 + 128],
                                      src[:, ss * 134:ss * 134 + 128],
                                      idf_sb[:]),
                      waits=first if s == 0 else [])
                    tlast = E("tensor",
                              lambda e, src=p_n[gp % 4], ss=s, bk=pbf[tbk]:
                                  e.transpose(bk[0:6, 256 + ss * 128:
                                                 256 + ss * 128 + 128],
                                              src[:, ss * 134 + 128:
                                                  ss * 134 + 134],
                                              idf_sb[:]))
                gd["tp"][gp] = [tlast]
                gd[("tpd", gp)] = tlast
            return tlast

        def at_evac(gl, t_tp):
            toks = []
            for i in (0, 1):
                gp = 32 * a + 2 * gl + i
                tbk = 6 if gp % 2 == 0 else 1
                w = [gd[("tpd", gp)]] + gd["avmp"].get(gp - 4, [])
                tc1 = E("scalar",
                        lambda e, d=at1[gp % 4], bk=pbf[tbk]:
                            e.activation(d[:], bk[:, 0:256], AF.Identity),
                        waits=w)
                tc2 = E("scalar",
                        lambda e, d=at2[gp % 4], bk=pbf[tbk]:
                            e.activation(d[0:6, :], bk[0:6, 256:512],
                                         AF.Identity),
                        waits=[tc1])
                gd["atw"][gp % 2] = [tc1, tc2]
                bank_free[tbk] = tc2
                gd[("atp", gp)] = [tc1, tc2]
            return toks

        def av_pair(pr, t_at):
            gp = 32 * a + pr
            gg = gp // 2
            jl, hp = pr // 8, pr % 8
            bi = 7 if gp % 2 == 0 else 0
            toks = []
            for s in range(2):
                hd = hp * 2 + s
                col = s * 128
                r0 = s * 64
                tpos = (0, 64) if s == 1 else None
                toks.append(E("tensor",
                              lambda e, t=pbf[bi], jj=jl, hh=hd, cl=col, r=r0,
                                     tp=tpos, ab=at1[gp % 4]:
                                  e.matmul(t[r:r + 64, 0:128],
                                           v_tok[:, jj * H + hh * 64:
                                                 jj * H + hh * 64 + 64],
                                           ab[:, cl:cl + 128],
                                           start=True, stop=False,
                                           tile_position=tp,
                                           skip_group_check=True),
                              waits=([bank_free[bi]] + gd[("atp", gp)] + t_vev
                                     if s == 0 else [])))
                toks.append(E("tensor",
                              lambda e, t=pbf[bi], jj=jl, hh=hd, cl=col, r=r0,
                                     tp=tpos, ab=at2[gp % 4]:
                                  e.matmul(t[r:r + 64, 0:128],
                                           v_tok[0:6, (jj + 1) * H + hh * 64:
                                                 (jj + 1) * H + hh * 64 + 64],
                                           ab[0:6, cl:cl + 128],
                                           start=False, stop=True,
                                           tile_position=tp,
                                           skip_group_check=True)))
            bank_done[bi] = toks[-1]
            gd["avmp"][gp] = [toks[-1]]
            gd[("avmm", gp)] = toks
            bank_free[bi] = E("scalar",
                              lambda e, t=pbf[bi], c=hp, jj=jl:
                                  e.activation(o_fm[c][:, jj * 128:
                                                       jj * 128 + 128],
                                               t[:, 0:128], AF.Identity),
                              waits=[bank_done[bi]] + list(gd["ofm"]))
            gd[("oev", gp)] = bank_free[bi]

        def xres_prefetch(jl):
            gqb = 4 * a + jl
            par = jl % 2
            xw = list(gd["xres"][par])
            if gd["first_xres"]:
                xw += list(t_ln_all)
                gd["first_xres"] = False
            gd[("xr", jl)] = E("sync",
                               lambda e, g=gqb, d=x_res[par]:
                                   e.dma_start(d[:],
                                               x_tok_h[3 + g * 128:
                                                       3 + g * 128 + 128, :]),
                               waits=xw, stream=f"dR{par}")

        def oproj(jl):
            gqb = 4 * a + jl
            par = jl % 2
            oev = [gd[("oev", 32 * a + 8 * jl + k)] for k in range(8)]
            t_xr = gd[("xr", jl)]
            accts = []
            for half in range(2):
                bi = 0
                for ic in range(8):
                    E("tensor",
                      lambda e, i=ic, t=pbf[bi], jj=jl, hh=half:
                          e.matmul(t[:], o_fm[i][:, jj * 128:jj * 128 + 128],
                                   wo_sb[i][:, hh * 512:(hh + 1) * 512],
                                   start=(i == 0), stop=False,
                                   skip_group_check=True),
                      waits=([bank_free[bi]] + oev + t_beff + [t_cAll]
                             if ic == 0 else []))
                tk = E("tensor",
                       lambda e, t=pbf[bi], hh=half:
                           e.matmul(t[:], ones_sb[0:1, :],
                                    beff_sb[0:1, hh * 512:(hh + 1) * 512],
                                    start=False, stop=True,
                                    skip_group_check=True),
                       waits=[])
                bank_done[bi] = tk
                tv = E("vector",
                       lambda e, t=pbf[bi], g=gqb, hh=half, xr=x_res[par]:
                           e.tensor_tensor(acc[:, g * H + hh * 512:
                                               g * H + hh * 512 + 512],
                                           t[:], xr[:, hh * 512:hh * 512 + 512],
                                           ALU.add),
                       waits=[bank_done[bi], t_xr])
                bank_free[bi] = tv
                accts.append(tv)
            gd["xres"][par] = accts
            gd[("acc", gqb)] = accts

        if upto in ("sm", "tp", "av"):
            sc_pair(0)
            sm_pair(0)
            sc_pair(1)
            sm_pair(1)
            if upto in ("tp", "av"):
                t_tp = tp_group(0)
                t_at = at_evac(0, t_tp)
                if upto == "av":
                    av_pair(0, t_at)
                    av_pair(1, t_at)
            return
        for g in range(18):
            if 1 <= g <= 16:
                gl = g - 1
                t_tp = tp_group(gl)
                at_evac(gl, t_tp)
            if g < 16:
                sc_pair(2 * g)
                sm_pair(2 * g)
                sc_pair(2 * g + 1)
                sm_pair(2 * g + 1)
            if g >= 2:
                ga = g - 2
                if ga % 4 == 0:
                    xres_prefetch(ga // 4)
                av_pair(2 * ga, None)
                av_pair(2 * ga + 1, None)
                if (2 * ga + 1) % 8 == 7:
                    oproj((2 * ga + 1) // 8)

        gd["qkv"] = [p.last("tensor")]
        gd["ofm"] = [p.last("tensor")]

    if stage in ("ln", "ln0"):
        dump_and_finish("xn")
    if stage in ("a1", "a2", "a3", "a4"):
        emit_atile(0, {"a1": "qkv", "a2": "sm", "a3": "tp", "a4": "av"}[stage])
        base = p.barrier_tokens()
        dumps = []
        if stage == "a1":
            for ic in range(8):
                dumps.append((y_h[ic * 128:(ic + 1) * 128, 0:512], q_fm[ic][:]))
                dumps.append((y_h[ic * 128:(ic + 1) * 128, 512:1024],
                              k_fm[ic][:, 0:512]))
            for tb in range(5):
                dumps.append((y_h[1024 + tb * 128:1152 + tb * 128, :],
                              v_tok[:, tb * H:(tb + 1) * H]))
        elif stage == "a2":
            dumps.append((y_h[0:128, 0:268], s_m[0][:]))
            dumps.append((y_h[128:256, 0:268], p_sb[0][:]))
            dumps.append((y_h[256:384, 0:268], p_n[0][:]))
            dumps.append((y_h[384:512, 0:2], nm_sb[0][:]))
            dumps.append((y_h[512:640, 0:2], z_sb[0][:]))
            dumps.append((y_h[640:768, 0:2], rz_sb[0][:]))
            dumps.append((y_h[768:896, 0:268], s_m[1][:]))
            dumps.append((y_h[896:1024, 0:268], p_n[1][:]))
        elif stage == "a3":
            dumps.append((y_h[0:128, 0:512], at1[0][:]))
            dumps.append((y_h[128:134, 0:512], at2[0][0:6, :]))
        elif stage == "a4":
            dumps.append((y_h[0:128, 0:512], o_fm[0][:]))
            dumps.append((y_h[128:256, 0:512], o_fm[1][:]))
        tprev = None
        for dst, srcap in dumps:
            tprev = E("gpsimd", lambda e, d=dst, s=srcap: e.dma_start(d, s),
                      waits=list(base) + ([tprev] if tprev else []),
                      stream="dY0")
        t_last_y[0] = tprev
    if stage in ("full", "attn", "b1", "b2g"):
        for a in range(NA):
            emit_atile(a)
    if stage == "attn":
        dump_and_finish("acc")

    t_phA = p.barrier_tokens()

    g_wg = [[], []]
    g_wu = [[], []]
    hm_prev = [None, None, None, None]
    nhc = (HC if stage in ("full", "b1", "b2g")
           else (1 if stage == "b1g" else 0))
    if nhc:
        E("scalar",
          lambda e: e.add_instruction(mybir.InstLoadActFuncSet(
              name=nc.get_next_instruction_name(),
              act_func_set_id=18, ins=[], outs=[])),
          waits=t_phA)
    for hc in range(nhc):
        par = hc % 2
        t_wg = E("gpsimd",
                 lambda e, c=hc, d=wgst[par]:
                     e.dma_start(d[:], wg_hh[c * 128:(c + 1) * 128, :]),
                 waits=list(g_wg[par]) + (t_phA if hc == 0 else []),
                 stream=f"dG{par}")
        t_wu = E("gpsimd",
                 lambda e, c=hc, d=wust[par]:
                     e.dma_start(d[:], wu_hh[c * 128:(c + 1) * 128, :]),
                 waits=list(g_wu[par]), stream=f"dU{par}")
        gate_last = []
        for ic in range(8):
            for g in range(4):
                bi = g
                tk = E("tensor",
                       lambda e, i=ic, gg=g, t=pbf[bi], w=wgst[par]:
                           e.matmul(t[:], w[:, i * 128:(i + 1) * 128],
                                    xn[i][:, 3 + gg * 512:3 + gg * 512 + 512],
                                    start=(i == 0), stop=(i == 7),
                                    skip_group_check=True),
                       waits=(([bank_free[bi]] if ic == 0 else []) +
                              ([t_wg] + (t_phA if hc == 0 else [])
                               if (ic == 0 and g == 0) else [])))
                if ic == 7:
                    bank_done[bi] = tk
                    gate_last.append(tk)
        g_wg[par] = [gate_last[-1]]
        t_silu = []
        for g in range(4):
            bi = g
            ts_ = E("scalar",
                    lambda e, gg=g, t=pbf[bi]:
                        e.activation(htmp[:, gg * 512:gg * 512 + 512], t[:],
                                     AF.Silu, scale=QS),
                    waits=[bank_done[bi], hm_prev[g]] +
                          (t_phA if hc == 0 and g == 0 else []))
            bank_free[bi] = ts_
            t_silu.append(ts_)
        up_last = []
        for ic in range(8):
            for g in range(4):
                bi = 4 + g
                tk = E("tensor",
                       lambda e, i=ic, gg=g, t=pbf[bi], w=wust[par]:
                           e.matmul(t[:], w[:, i * 128:(i + 1) * 128],
                                    xn[i][:, 3 + gg * 512:3 + gg * 512 + 512],
                                    start=(i == 0), stop=(i == 7),
                                    skip_group_check=True),
                       waits=(([bank_free[bi]] if ic == 0 else []) +
                              ([t_wu] if (ic == 0 and g == 0) else [])))
                if ic == 7:
                    bank_done[bi] = tk
                    up_last.append(tk)
        g_wu[par] = [up_last[-1]]
        for g in range(4):
            bi = 4 + g
            bank_free[bi] = E("vector",
                              lambda e, c=hc, gg=g, t=pbf[bi]:
                                  e.tensor_tensor(
                                      h_sb[c][:, gg * 512:gg * 512 + 512],
                                      htmp[:, gg * 512:gg * 512 + 512],
                                      t[:], ALU.mult),
                              waits=[bank_done[bi], t_silu[g]] +
                                    (t_phA if hc == 0 and g == 0 else []))
            hm_prev[g] = bank_free[bi]

    if stage == "b1":
        dump_and_finish("h")
    if stage == "b1g":
        base = p.barrier_tokens()
        dumps = [(0, wgst[0][:, 0:1024]),
                 (1, htmp[:, 0:1024]),
                 (2, h_sb[0][:, 0:1024]),
                 (3, xn[0][:, 0:1024]),
                 (4, pbf[0][:, 0:512]),
                 (5, pbf[4][:, 0:512])]
        tprev = None
        for row, srcap in dumps:
            tv = E("vector",
                   lambda e, s=srcap, n=srcap.shape[-1]:
                       e.tensor_copy(y_sb[0][:, 0:n], s),
                   waits=list(base) + ([tprev] if tprev else []))
            tprev = E("sync",
                      lambda e, rr=row: e.dma_start(
                          y_h[rr * 128:(rr + 1) * 128, :], y_sb[0][:]),
                      waits=[tv], stream="dY0")
        t_last_y[0] = tprev

    t_phB = p.barrier_tokens()

    t_wd = []
    for hc in range(HC if stage in ("full", "b2g") else 0):
        t_wd.append(E("gpsimd",
                      lambda e, c=hc:
                          e.dma_start(wdT[c][:],
                                      wd_hh[c * 128:(c + 1) * 128, :]),
                      waits=(t_phB if hc == 0 else []),
                      stream="dD"))

    t_ystore = [None, None]
    for tb in range(NQB if stage == "full" else (1 if stage == "b2g" else 0)):
        evs = []
        for half in range(2):
            bi = (tb % 2) * 2 + half
            tk = None
            for hc in range(HC):
                w = []
                if hc == 0:
                    w = [bank_free[bi], t_wd[-1]] + (t_phB
                                                     if tb == 0 and half == 0
                                                     else [])
                tk = E("tensor",
                       lambda e, c=hc, tt=tb, hh=half, t=pbf[bi]:
                           e.matmul(t[:], h_sb[c][:, tt * 128:tt * 128 + 128],
                                    wdT[c][:, hh * 512:(hh + 1) * 512],
                                    start=(c == 0), stop=(c == HC - 1),
                                    skip_group_check=True),
                       waits=w)
            bank_done[bi] = tk
            tv = E("vector",
                   lambda e, tt=tb, hh=half, t=pbf[bi], d=y_sb[tb % 2]:
                       e.scalar_tensor_tensor(
                           d[:, hh * 512:hh * 512 + 512], t[:], QS2,
                           acc[:, tt * H + hh * 512:tt * H + hh * 512 + 512],
                           ALU.mult, ALU.add),
                   waits=[bank_done[bi]] +
                         ([t_ystore[tb % 2]] if t_ystore[tb % 2] else []))
            bank_free[bi] = tv
            evs.append(tv)
        t_ystore[tb % 2] = E("sync",
                             lambda e, tt=tb, s=y_sb[tb % 2]:
                                 e.dma_start(y_h[tt * 128:tt * 128 + 128, :],
                                             s[:]),
                             waits=evs, stream=f"dY{tb % 2}")
        if tb == NQB - 1 or stage == "b2g":
            t_last_y[0] = t_ystore[tb % 2]
    if stage == "b2g":
        base = p.barrier_tokens()
        tprev = None
        for row, srcap in ((1, wdT[0][:, 0:1024]), (2, wdT[23][:, 0:1024]),
                           (3, acc[:, 0:1024]), (4, h_sb[0][:, 0:1024])):
            tv = E("vector",
                   lambda e, s=srcap: e.tensor_copy(y_sb[1][:], s),
                   waits=list(base) + ([tprev] if tprev else []))
            tprev = E("sync",
                      lambda e, rr=row: e.dma_start(
                          y_h[rr * 128:(rr + 1) * 128, :], y_sb[1][:]),
                      waits=[tv], stream="dY1")
        t_last_y[0] = tprev

    final_tok = t_last_y[0]
    all_keys = set(p.nkey)
    dma_keys = all_keys - set(("tensor", "vector", "scalar"))
    waited = {k: set() for k in all_keys}
    for k in all_keys:
        if k not in dma_keys:
            waited[k] = set(range(p.nkey[k]))
    for e in ENGINES:
        for _, ws, _k in p.ops[e]:
            for (wk_, wi) in ws:
                waited[wk_].add(wi)
    waited[final_tok[0]].add(final_tok[1])
    rank = {}
    for k in all_keys:
        if k in dma_keys:
            rank[k] = {i: i + 1 for i in range(p.nkey[k])}
        else:
            rank[k] = {i: r + 1 for r, i in enumerate(sorted(waited[k]))}
    unit = {k: (16 if k in dma_keys else 1) for k in all_keys}

    import contextlib
    with contextlib.ExitStack() as stk:
        semh = {k: stk.enter_context(nc.semaphore("c_" + str(k)))
                for k in all_keys}
        kcnt = {k: 0 for k in all_keys}
        with nc.Block() as block:
            def make(eng_name):
                def run(e):
                    if eng_name == "scalar":
                        e.add_instruction(mybir.InstLoadActFuncSet(
                            name=nc.get_next_instruction_name(),
                            act_func_set_id=6, ins=[], outs=[]))
                    for fn, ws, key in p.ops[eng_name]:
                        for (wk_, wi) in ws:
                            e.wait_ge(semh[wk_], rank[wk_][wi] * unit[wk_])
                        inst = fn(e)
                        i = kcnt[key]
                        kcnt[key] += 1
                        if key in dma_keys or i in waited[key]:
                            inst.then_inc(semh[key], unit[key])
                    if eng_name == "sync":
                        e.wait_ge(semh[final_tok[0]],
                                  rank[final_tok[0]][final_tok[1]] *
                                  unit[final_tok[0]])
                return run

            block.sync(make("sync"))
            block.gpsimd(make("gpsimd"))
            block.tensor(make("tensor"))
            block.vector(make("vector"))
            block.scalar(make("scalar"))
    return nc


_NC = None


def _get_nc():
    global _NC
    if _NC is None:
        _NC = build_nc()
    return _NC


def _band_mask(first, last):
    m = np.full((128, 134), NEG, np.float32)
    for r in range(128):
        lo = max(r, 3) if first else r
        hi = min(r + 6, 130) if last else r + 6
        m[r, lo:hi + 1] = 0.0
    return np.concatenate([m, m], axis=1).astype(ml_dtypes.bfloat16)


def kernel(L0_attn_hidden_states, L0_attn_w_q, L0_attn_b_q, L0_attn_w_k,
           L0_attn_b_k, L0_attn_w_v, L0_attn_b_v, L0_attn_w_o, L0_attn_b_o,
           L0_ffn_w_gate, L0_ffn_w_up, L0_ffn_w_down):
    x = np.ascontiguousarray(np.asarray(L0_attn_hidden_states, np.float32))
    wq = np.ascontiguousarray(np.asarray(L0_attn_w_q, np.float32))
    wk = np.ascontiguousarray(np.asarray(L0_attn_w_k, np.float32))
    wv = np.ascontiguousarray(np.asarray(L0_attn_w_v, np.float32))
    wo = np.ascontiguousarray(np.asarray(L0_attn_w_o, np.float32))
    bq = np.asarray(L0_attn_b_q, np.float32)
    bk = np.asarray(L0_attn_b_k, np.float32)
    bv = np.asarray(L0_attn_b_v, np.float32)
    bo = np.asarray(L0_attn_b_o, np.float32)
    wg = np.asarray(L0_ffn_w_gate, np.int32)
    wu = np.asarray(L0_ffn_w_up, np.int32)
    wd = np.asarray(L0_ffn_w_down, np.int32)

    wg_b = np.ascontiguousarray(
        wg.reshape(HC, 128, 8, 128).transpose(0, 3, 2, 1).reshape(FH, H))
    wu_b = np.ascontiguousarray(
        wu.reshape(HC, 128, 8, 128).transpose(0, 3, 2, 1).reshape(FH, H))
    wd_b = np.ascontiguousarray(wd.T)

    shared = dict(
        wq=wq, wk=wk, wv=wv, wo=wo,
        bq=np.ascontiguousarray(bq.reshape(8, 128).T),
        bk=np.ascontiguousarray(bk.reshape(8, 128).T),
        bv=np.ascontiguousarray(bv.reshape(8, 128).T),
        bo_row=np.ascontiguousarray(bo.reshape(1, H)),
        wg_b=wg_b, wu_b=wu_b, wd_b=wd_b,
        ident=np.eye(128, dtype=ml_dtypes.bfloat16),
        ident_f=np.eye(128, dtype=np.float32),
        ones_row=np.ones((1, 128), dtype=ml_dtypes.bfloat16),
        eps_col=np.full((128, 1), EPS, np.float32),
    )
    m_mid = _band_mask(False, False)
    m_first = _band_mask(True, False)
    m_last = _band_mask(False, True)

    in_maps = []
    for core in range(NCORES):
        b_i, qi = core // 4, core % 4
        t0 = qi * TOK
        x_ext = np.zeros((EXT, H), np.float32)
        lo = t0 - 3
        s0, s1 = max(0, lo), min(S, lo + EXT)
        x_ext[s0 - lo:s1 - lo] = x[b_i, s0:s1]
        im = dict(shared)
        im["x_tok"] = x_ext
        im["mask_first"] = m_first if qi == 0 else m_mid
        im["mask_mid"] = m_mid
        im["mask_last"] = m_last if qi == 3 else m_mid
        in_maps.append(im)

    res = run_bass_kernel_spmd(_get_nc(), in_maps, core_ids=list(range(NCORES)))
    y = np.empty((B, S, H), np.float32)
    for core in range(NCORES):
        b_i, qi = core // 4, core % 4
        y[b_i, qi * TOK:(qi + 1) * TOK] = res.results[core]["y"]
    return y


if __name__ == "__main__":
    np.random.seed(0)
    import reference as ref
    ins = {k: np.asarray(v) for k, v in ref.setup_inputs().items()}
    out = kernel(**ins)
    exp = np.asarray(ref.reference(**ins))
    print("rvr:", float(np.mean((out - exp) ** 2) / np.mean(exp ** 2)))
    print("maxabs:", float(np.abs(out - exp).max()))


# version 16 (final)
# speedup vs baseline: 1.0321x; 1.0091x over previous
import numpy as np
import ml_dtypes
import concourse.bass as bass
import concourse.mybir as mybir
from concourse.bass_utils import run_bass_kernel_spmd

F32 = mybir.dt.float32
BF16 = mybir.dt.bfloat16
I32 = mybir.dt.int32
AX = mybir.AxisListType.X
AF = mybir.ActivationFunctionType
ALU = mybir.AluOpType

B, S, H = 2, 8192, 1024
NCORES = 8
TOK = 2048
EXT = 2176
NTILE = 17
SCALE = 0.125
FH, HC = 3072, 24
QS = 1.0 / 127.0
QS2 = QS * QS
NA = 4
NQB = 16
EPS = 1e-5
NEG = -1e30

ENGINES = ("sync", "gpsimd", "tensor", "vector", "scalar")
DMA_ENG = {"sync": True, "gpsimd": True, "tensor": False, "vector": False,
           "scalar": False}


class Prog:

    def __init__(self):
        self.ops = {e: [] for e in ENGINES}
        self.nkey = {}
        self.last_of = {}

    def emit(self, eng, fn, waits=(), stream=None):
        key = stream if stream is not None else eng
        idx = self.nkey.get(key, 0)
        self.nkey[key] = idx + 1
        tok = (key, idx)
        self.ops[eng].append((fn, [w for w in waits if w is not None], key))
        self.last_of[key] = tok
        return tok

    def last(self, eng):
        return self.last_of.get(eng)

    def barrier_tokens(self):
        return list(self.last_of.values())


def build_nc(stage="full"):
    nc = bass.Bass("TRN2", target_bir_lowering=False)
    p = Prog()

    x_tok_h = nc.dram_tensor("x_tok", [EXT, H], F32, kind="ExternalInput").ap()
    wq_h = nc.dram_tensor("wq", [H, H], F32, kind="ExternalInput").ap()
    wk_h = nc.dram_tensor("wk", [H, H], F32, kind="ExternalInput").ap()
    wv_h = nc.dram_tensor("wv", [H, H], F32, kind="ExternalInput").ap()
    wo_h = nc.dram_tensor("wo", [H, H], F32, kind="ExternalInput").ap()
    bq_h = nc.dram_tensor("bq", [128, 8], F32, kind="ExternalInput").ap()
    bk_h = nc.dram_tensor("bk", [128, 8], F32, kind="ExternalInput").ap()
    bv_h = nc.dram_tensor("bv", [128, 8], F32, kind="ExternalInput").ap()
    bo_h = nc.dram_tensor("bo_row", [1, H], F32, kind="ExternalInput").ap()
    wg_hh = nc.dram_tensor("wg_b", [FH, H], I32, kind="ExternalInput").ap()
    wu_hh = nc.dram_tensor("wu_b", [FH, H], I32, kind="ExternalInput").ap()
    wd_hh = nc.dram_tensor("wd_b", [FH, H], I32, kind="ExternalInput").ap()
    mskf_h = nc.dram_tensor("mask_first", [128, 268], BF16, kind="ExternalInput").ap()
    mskm_h = nc.dram_tensor("mask_mid", [128, 268], BF16, kind="ExternalInput").ap()
    mskl_h = nc.dram_tensor("mask_last", [128, 268], BF16, kind="ExternalInput").ap()
    idn_h = nc.dram_tensor("ident", [128, 128], BF16, kind="ExternalInput").ap()
    idf_h = nc.dram_tensor("ident_f", [128, 128], F32, kind="ExternalInput").ap()
    ones_h = nc.dram_tensor("ones_row", [1, 128], BF16, kind="ExternalInput").ap()
    eps_h = nc.dram_tensor("eps_col", [128, 1], F32, kind="ExternalInput").ap()
    y_h = nc.dram_tensor("y", [TOK, H], F32, kind="ExternalOutput").ap()

    SBUF_BASE = 16640
    SBUF_LIMIT = 229376
    _off = [SBUF_BASE]

    def at(name, shape, dtype, off=None):
        nb = shape[-1] * mybir.dt.size(dtype)
        if off is None:
            off = _off[0]
            _off[0] = off + ((nb + 255) // 256) * 256
            assert _off[0] <= SBUF_LIMIT, (name, _off[0])
        return nc.alloc_sbuf_tensor_at(name, list(shape), dtype, offset=off)

    mskf = at("mskf", [128, 268], BF16)
    mskm = at("mskm", [128, 268], BF16)
    mskl = at("mskl", [128, 268], BF16)
    bq_sb = at("bq_sb", [128, 8], F32)
    bk_sb = at("bk_sb", [128, 8], F32)
    bv_sb = at("bv_sb", [128, 8], BF16)
    bqs_sb = at("bqs_sb", [128, 8], F32)
    bo_sb = at("bo_sb", [1, H], F32)
    beff_sb = at("beff_sb", [1, H], BF16)
    idn_sb = at("idn_sb", [128, 128], BF16)
    idf_sb = at("idf_sb", [128, 128], F32)
    ones_sb = at("ones_sb", [1, 128], BF16)
    eps_sb = at("eps_sb", [128, 1], F32)
    st = {k: at("st_" + k, [128, NTILE], F32)
          for k in ("s1", "s2", "ms", "m2", "va", "ln", "a", "b")}
    acc = at("acc", [128, NQB * H], BF16)
    y_sb = [at(f"y_sb{i}", [128, H], F32) for i in range(2)]

    xn0 = _off[0]
    xn = [at(f"xn{ic}", [128, EXT], BF16) for ic in range(8)]
    h0 = _off[0]
    h_sb = [at(f"h{c}", [128, TOK], BF16) for c in range(HC)]
    top0 = _off[0]
    wgst = [at(f"wgst{i}", [128, H], BF16) for i in range(2)]
    wust = [at(f"wust{i}", [128, H], BF16) for i in range(2)]
    htmp = at("htmp", [128, TOK], BF16)
    assert _off[0] <= SBUF_LIMIT

    wdT = []
    for c in range(HC):
        off = xn0 + 2048 * c if c < 17 else top0 + 2048 * (c - 17)
        wdT.append(at(f"wdT{c}", [128, H], BF16, off=off))

    a_off = [h0]

    def atA(name, shape, dtype):
        nb = shape[-1] * mybir.dt.size(dtype)
        off = a_off[0]
        a_off[0] = off + ((nb + 255) // 256) * 256
        assert a_off[0] <= SBUF_LIMIT, (name, a_off[0])
        return nc.alloc_sbuf_tensor_at(name, list(shape), dtype, offset=off)

    wq_sb = [atA(f"wqb{i}", [128, H], BF16) for i in range(8)]
    wk_sb = [atA(f"wkb{i}", [128, H], BF16) for i in range(8)]
    wv_sb = [atA(f"wvb{i}", [128, H], BF16) for i in range(8)]
    wo_sb = [atA(f"wob{i}", [128, H], BF16) for i in range(8)]
    ln0 = a_off[0]
    q_fm = [atA(f"qfm{i}", [128, 512], BF16) for i in range(8)]
    k_fm = [atA(f"kfm{i}", [128, 528], BF16) for i in range(8)]
    v_tok = atA("v_tok", [128, 5 * H], BF16)
    o_fm = [atA(f"ofm{i}", [128, 512], BF16) for i in range(8)]
    p_sb = [atA(f"p_sb{i}", [128, 268], BF16) for i in range(2)]
    p_n = [atA(f"p_n{i}", [128, 268], F32) for i in range(4)]
    nm_sb = [atA(f"nm{i}", [128, 2], F32) for i in range(2)]
    z_sb = [atA(f"z{i}", [128, 2], F32) for i in range(2)]
    rz_sb = [atA(f"rz{i}", [128, 2], F32) for i in range(2)]
    at1 = [atA(f"at1_{i}", [128, 256], BF16) for i in range(4)]
    at2 = [atA(f"at2_{i}", [6, 256], BF16) for i in range(4)]
    x_res = [atA(f"xres{i}", [128, H], F32) for i in range(2)]
    lnoff = [ln0]

    def atLN(name, shape, dtype):
        nb = shape[-1] * mybir.dt.size(dtype)
        off = lnoff[0]
        lnoff[0] = off + ((nb + 255) // 256) * 256
        return nc.alloc_sbuf_tensor_at(name, list(shape), dtype, offset=off)

    xst = [atLN(f"xst{i}", [128, H], F32) for i in range(2)]
    xnst = [atLN(f"xnst{i}", [128, H], F32) for i in range(2)]
    sq_scr = atLN("sq_scr", [128, H], BF16)

    pb = [nc.alloc_psum_tensor(f"pb{i}", [128, 512], F32) for i in range(8)]
    pbf = [t.ap() for t in pb]

    E = p.emit

    t_consts = []
    for dst, src in ((mskf, mskf_h), (mskm, mskm_h), (mskl, mskl_h),
                     (bq_sb, bq_h), (bk_sb, bk_h),
                     (bo_sb, bo_h), (idn_sb, idn_h), (idf_sb, idf_h),
                     (ones_sb, ones_h),
                     (eps_sb, eps_h)):
        t_consts.append(E("sync", lambda e, d=dst, s=src: e.dma_start(d[:], s[:]),
                          stream="dC"))
    t_cAll = t_consts[-1]
    t_bv = E("gpsimd", lambda e: e.dma_start(bv_sb[:], bv_h[:]), stream="dBv")

    t_w = {}
    wlist = (() if stage == "ln0" else
             (("q", wq_sb, wq_h), ("k", wk_sb, wk_h),
              ("v", wv_sb, wv_h), ("o", wo_sb, wo_h)))
    for nm, bufs, src in wlist:
        t = None
        for ic in range(8):
            t = E("gpsimd",
                  lambda e, d=bufs[ic], s=src, i=ic:
                      e.dma_start(d[:], s[i * 128:(i + 1) * 128, :]),
                  stream="dW" + nm)
        t_w[nm] = t

    t_bqs = E("vector", lambda e: e.tensor_scalar_mul(bqs_sb[:], bq_sb[:], SCALE),
              waits=[t_cAll])

    t_beff = []
    bank_free = [None] * 8
    bank_done = [None] * 8
    for half in range(2 if stage != "ln0" else 0):
        tk = None
        for ic in range(8):
            tk = E("tensor",
                   lambda e, i=ic, t=pbf[half], hh=half:
                       e.matmul(t[0:1, :], bv_sb[:, i:i + 1],
                                wo_sb[i][:, hh * 512:(hh + 1) * 512],
                                start=(i == 0), stop=(i == 7),
                                skip_group_check=True),
                   waits=([t_w["o"], t_bv] if ic == 0 else []))
        bank_done[half] = tk
        tv = E("vector",
               lambda e, t=pbf[half], hh=half:
                   e.tensor_tensor(beff_sb[0:1, hh * 512:(hh + 1) * 512],
                                   t[0:1, :], bo_sb[0:1, hh * 512:(hh + 1) * 512],
                                   ALU.add),
               waits=[bank_done[half], t_cAll])
        bank_free[half] = tv
        t_beff.append(tv)

    inv_h = 1.0 / H
    xst_guard = [[], []]
    xnst_guard = [[], []]
    ln_bank_guard = [[], []]
    ln_tile_tok = []
    for i in range(NTILE):
        par = i % 2
        t_x = E("sync",
                lambda e, j=i, d=xst[par]:
                    e.dma_start(d[:], x_tok_h[j * 128:(j + 1) * 128, :]),
                waits=xst_guard[par], stream=f"dX{par}")
        t_s1 = E("vector",
                 lambda e, j=i, s=xst[par]:
                     e.reduce_sum(st["s1"][:, j:j + 1], s[:], axis=AX),
                 waits=[t_x])
        t_sq = E("scalar",
                 lambda e, j=i, s=xst[par]:
                     e.activation(sq_scr[:], s[:], AF.Square,
                                  accum_out=st["s2"][:, j:j + 1]),
                 waits=[t_x])
        t_ms = E("vector",
                 lambda e, j=i:
                     e.tensor_scalar_mul(st["ms"][:, j:j + 1],
                                         st["s1"][:, j:j + 1], inv_h),
                 waits=[t_s1])
        t_m2 = E("vector",
                 lambda e, j=i:
                     e.tensor_mul(st["m2"][:, j:j + 1], st["ms"][:, j:j + 1],
                                  st["ms"][:, j:j + 1]),
                 waits=[t_ms])
        t_va = E("vector",
                 lambda e, j=i:
                     e.scalar_tensor_tensor(st["va"][:, j:j + 1],
                                            st["s2"][:, j:j + 1], inv_h,
                                            st["m2"][:, j:j + 1],
                                            ALU.mult, ALU.subtract),
                 waits=[t_sq, t_m2])
        t_ln = E("scalar",
                 lambda e, j=i:
                     e.activation(st["ln"][:, j:j + 1], st["va"][:, j:j + 1],
                                  AF.Ln, bias=eps_sb[:, 0:1]),
                 waits=[t_va, t_cAll])
        t_a = E("scalar",
                lambda e, j=i:
                    e.activation(st["a"][:, j:j + 1], st["ln"][:, j:j + 1],
                                 AF.Exp, scale=-0.5),
                waits=[t_ln])
        t_b = E("vector",
                lambda e, j=i:
                    e.tensor_scalar(st["b"][:, j:j + 1], st["ms"][:, j:j + 1],
                                    st["a"][:, j:j + 1], -1.0,
                                    ALU.mult, ALU.mult),
                waits=[t_a])
        t_ap = E("scalar",
                 lambda e, j=i, s=xst[par], d=xnst[par]:
                     e.activation(d[:], s[:], AF.Identity,
                                  bias=st["b"][:, j:j + 1],
                                  scale=st["a"][:, j:j + 1]),
                 waits=[t_b] + xnst_guard[par])
        xst_guard[par] = [t_sq, t_ap]
        t_tr = None
        for ic in range(8):
            t_tr = E("tensor",
                     lambda e, j=i, c=ic, s=xnst[par], bk=pbf[6 + (ic // 4)]:
                         e.transpose(bk[:, (c % 4) * 128:(c % 4) * 128 + 128],
                                     s[:, c * 128:(c + 1) * 128], idf_sb[:]),
                     waits=([t_ap, t_cAll] + ln_bank_guard[0]
                            if ic == 0 else []))
        xnst_guard[par] = [t_tr]
        evs = []
        for ic in range(8):
            eng = "vector" if ic < 4 else "scalar"
            fn = ((lambda e, j=i, c=ic, bk=pbf[6]:
                       e.tensor_copy(xn[c][:, j * 128:(j + 1) * 128],
                                     bk[:, (c % 4) * 128:(c % 4) * 128 + 128]))
                  if eng == "vector" else
                  (lambda e, j=i, c=ic, bk=pbf[7]:
                       e.activation(xn[c][:, j * 128:(j + 1) * 128],
                                    bk[:, (c % 4) * 128:(c % 4) * 128 + 128],
                                    AF.Identity)))
            evs.append(E(eng, fn, waits=[t_tr] if ic in (0, 4) else []))
        ln_bank_guard[0] = [evs[3], evs[7]]
        ln_tile_tok.append([evs[3], evs[7]])
    t_ln_all = p.barrier_tokens()

    gd = {"exps": {}, "pn": {}, "tp": {}, "xres": [[], []],
          "qkv": [], "ofm": [], "first_xres": True,
          "avm": {}, "avmp": {}, "atw": [[], []], "att": [None, None],
          "xr": {}}
    t_last_y = [None]

    def dump_and_finish(kind):
        tprev = None
        base = p.barrier_tokens()
        if kind == "xn":
            for ki, kk in enumerate(("s1", "s2", "ms", "m2", "va", "ln", "a", "b")):
                tprev = E("sync",
                          lambda e, k=ki, t=st[kk]:
                              e.dma_start(y_h[1024 + 128 * k:1152 + 128 * k, 0:NTILE],
                                          t[:]),
                          waits=list(base) + ([tprev] if tprev else []),
                          stream="dY0")
        n = 8 if kind == "xn" else NQB
        for i in range(n):
            w = list(base) + ([tprev] if tprev else [])
            if kind == "xn":
                tv = E("vector",
                       lambda e, c=i, d=y_sb[i % 2]:
                           e.tensor_copy(d[:], xn[c][:, 0:H]), waits=w)
            elif kind == "acc":
                tv = E("vector",
                       lambda e, t=i, d=y_sb[i % 2]:
                           e.tensor_copy(d[:], acc[:, t * H:(t + 1) * H]),
                       waits=w)
            else:
                tv = E("vector",
                       lambda e, c=i, d=y_sb[i % 2]:
                           e.tensor_copy(d[:], h_sb[c][:, 0:H]), waits=w)
            tprev = E("sync",
                      lambda e, t=i, s=y_sb[i % 2]:
                          e.dma_start(y_h[t * 128:(t + 1) * 128, :], s[:]),
                      waits=[tv], stream=f"dY{i % 2}")
        t_last_y[0] = tprev

    def emit_atile(a, upto="full"):
        ext0 = 512 * a
        ln_wait = list(ln_tile_tok[min(4 * a + 4, NTILE - 1)])
        qev, kev = [], []

        for jc in range(8):
            bi = jc % 2
            tk = None
            for ic in range(8):
                tk = E("tensor",
                       lambda e, i=ic, j=jc, t=pbf[bi], e0=ext0:
                           e.matmul(t[:], wq_sb[i][:, j * 128:(j + 1) * 128],
                                    xn[i][:, e0 + 3:e0 + 515],
                                    start=(i == 0), stop=(i == 7),
                                    skip_group_check=True),
                       waits=([bank_free[bi], t_w["q"], t_bqs] + ln_wait
                              if ic == 0 else []))
            bank_done[bi] = tk
            bank_free[bi] = E("scalar",
                              lambda e, j=jc, t=pbf[bi]:
                                  e.activation(q_fm[j][:], t[:], AF.Identity,
                                               bias=bqs_sb[:, j:j + 1],
                                               scale=SCALE),
                              waits=[bank_done[bi]] + list(gd["qkv"]))
            qev.append(bank_free[bi])
        t_qev = [bank_free[0], bank_free[1]]
        for jc in range(8):
            for grp, (c0, cw) in enumerate(((0, 512), (512, 6))):
                bi = grp
                tk = None
                for ic in range(8):
                    tk = E("tensor",
                           lambda e, i=ic, j=jc, t=pbf[bi], e0=ext0,
                                  cc=c0, ww=cw:
                               e.matmul(t[:, 0:ww],
                                        wk_sb[i][:, j * 128:(j + 1) * 128],
                                        xn[i][:, e0 + cc:e0 + cc + ww],
                                        start=(i == 0), stop=(i == 7),
                                        skip_group_check=True),
                           waits=([bank_free[bi], t_w["k"], t_cAll]
                                  + ln_wait if ic == 0 else []))
                bank_done[bi] = tk
                bank_free[bi] = E("scalar",
                                  lambda e, j=jc, t=pbf[bi], cc=c0, ww=cw:
                                      e.activation(k_fm[j][:, cc:cc + ww],
                                                   t[:, 0:ww], AF.Identity,
                                                   bias=bk_sb[:, j:j + 1]),
                                  waits=[bank_done[bi]] + list(gd["qkv"]))
                if grp == 1:
                    kev.append(bank_free[bi])
        t_kev = [bank_free[0], bank_free[1]]
        for tb in range(5):
            for half in range(2):
                bi = half
                tk = None
                for ic in range(8):
                    tk = E("tensor",
                           lambda e, i=ic, t=pbf[bi], m=4 * a + tb, hh=half:
                               e.matmul(t[:],
                                        xn[i][:, m * 128:m * 128 + 128],
                                        wv_sb[i][:, hh * 512:(hh + 1) * 512],
                                        start=(i == 0), stop=(i == 7),
                                        skip_group_check=True),
                           waits=([bank_free[bi], t_w["v"]] + ln_wait
                                  if ic == 0 else []))
                bank_done[bi] = tk
                bank_free[bi] = E("scalar",
                                  lambda e, t=pbf[bi], s=tb, hh=half:
                                      e.activation(
                                          v_tok[:, s * H + hh * 512:
                                                s * H + hh * 512 + 512],
                                          t[:], AF.Identity),
                                  waits=[bank_done[bi]] + list(gd["qkv"]))
        t_vev = [bank_free[0], bank_free[1]]
        t_qkv = t_qev + t_kev + t_vev
        if upto == "qkv":
            return

        def msel(jl):
            g = 4 * a + jl
            return mskf if g == 0 else (mskl if g == NQB - 1 else mskm)

        def sc_pair(pr):
            gp = 32 * a + pr
            jl, hp = pr // 8, pr % 8
            m = msel(pr // 8)
            toks = []
            for s in range(2):
                bi = (2 if s == 0 else 4) + gp % 2
                hd = hp * 2 + s
                jc, off = hd // 2, (hd % 2) * 64
                toks.append(E("tensor",
                              lambda e, t=pbf[bi], jj=jl, c=jc, o=off:
                                  e.matmul(t[:, 0:134],
                                           q_fm[c][o:o + 64,
                                                   jj * 128:jj * 128 + 128],
                                           k_fm[c][o:o + 64,
                                                   jj * 128:jj * 128 + 134],
                                           start=True, stop=False,
                                           skip_group_check=True),
                              waits=[bank_free[bi]] +
                                    ([qev[hp], kev[hp]] if s == 0 else [])))
                toks.append(E("tensor",
                              lambda e, t=pbf[bi], mm=m, ss=s:
                                  e.matmul(t[:, 0:134], idn_sb[:],
                                           mm[:, ss * 134:ss * 134 + 134],
                                           start=False, stop=True,
                                           skip_group_check=True),
                              waits=[t_cAll] if gp < 10 else []))
                bank_done[bi] = toks[-1]
            gd[("scmm", gp)] = toks

        def sm_pair(pr):
            gp = 32 * a + pr
            par = gp % 2
            t_exps = []
            for s in range(2):
                sbi = (2 if s == 0 else 4) + gp % 2
                t_nm = E("vector",
                         lambda e, t=pbf[sbi], d=nm_sb[par], ss=s:
                             e.reduce_max(d[:, ss:ss + 1], t[:, 0:134],
                                          axis=AX, negate=True),
                         waits=[bank_done[sbi]] +
                               (gd["exps"].get(gp - 2, []) if s == 0 else []))
                t_exps.append(E("scalar",
                                lambda e, t=pbf[sbi], ss=s, d=p_sb[par],
                                       nn=nm_sb[par], zz=z_sb[par]:
                                    e.activation(d[:, ss * 134:ss * 134 + 134],
                                                 t[:, 0:134],
                                                 AF.Exp, bias=nn[:, ss:ss + 1],
                                                 accum_out=zz[:, ss:ss + 1]),
                                waits=[t_nm] +
                                      (gd["pn"].get(gp - 2, [])
                                       if s == 0 else [])))
                bank_free[sbi] = t_exps[-1]
            gd["exps"][gp] = t_exps
            t_rz = E("vector",
                     lambda e, zz=z_sb[par], rr=rz_sb[par]:
                         e.reciprocal(rr[:], zz[:]),
                     waits=list(t_exps))
            t_pn = []
            for s in range(2):
                t_pn.append(E("vector",
                              lambda e, ss=s, src=p_sb[par], d=p_n[gp % 4],
                                     rr=rz_sb[par]:
                                  e.tensor_scalar_mul(
                                      d[:, ss * 134:ss * 134 + 134],
                                      src[:, ss * 134:ss * 134 + 134],
                                      rr[:, ss:ss + 1]),
                              waits=([t_rz] + gd["tp"].get(gp - 4, [])
                                     if s == 0 else [])))
            gd["pn"][gp] = t_pn

        def tp_group(gl):
            tlast = None
            for i in (0, 1):
                gp = 32 * a + 2 * gl + i
                tbk = 6 if gp % 2 == 0 else 1
                first = [gd["pn"][gp][-1]] + gd["atw"][gp % 2]
                if gp == 0:
                    first.append(t_cAll)
                for s in range(2):
                    E("tensor",
                      lambda e, src=p_n[gp % 4], ss=s, bk=pbf[tbk]:
                          e.transpose(bk[:, ss * 128:ss * 128 + 128],
                                      src[:, ss * 134:ss * 134 + 128],
                                      idf_sb[:]),
                      waits=first if s == 0 else [])
                    tlast = E("tensor",
                              lambda e, src=p_n[gp % 4], ss=s, bk=pbf[tbk]:
                                  e.transpose(bk[0:6, 256 + ss * 128:
                                                 256 + ss * 128 + 128],
                                              src[:, ss * 134 + 128:
                                                  ss * 134 + 134],
                                              idf_sb[:]))
                gd["tp"][gp] = [tlast]
                gd[("tpd", gp)] = tlast
            return tlast

        def at_evac(gl, t_tp):
            toks = []
            for i in (0, 1):
                gp = 32 * a + 2 * gl + i
                tbk = 6 if gp % 2 == 0 else 1
                w = [gd[("tpd", gp)]] + gd["avmp"].get(gp - 4, [])
                tc1 = E("scalar",
                        lambda e, d=at1[gp % 4], bk=pbf[tbk]:
                            e.activation(d[:], bk[:, 0:256], AF.Identity),
                        waits=w)
                tc2 = E("scalar",
                        lambda e, d=at2[gp % 4], bk=pbf[tbk]:
                            e.activation(d[0:6, :], bk[0:6, 256:512],
                                         AF.Identity),
                        waits=[tc1])
                gd["atw"][gp % 2] = [tc1, tc2]
                bank_free[tbk] = tc2
                gd[("atp", gp)] = [tc1, tc2]
            return toks

        def av_pair(pr, t_at):
            gp = 32 * a + pr
            gg = gp // 2
            jl, hp = pr // 8, pr % 8
            bi = 7 if gp % 2 == 0 else 0
            toks = []
            for s in range(2):
                hd = hp * 2 + s
                col = s * 128
                r0 = s * 64
                tpos = (0, 64) if s == 1 else None
                toks.append(E("tensor",
                              lambda e, t=pbf[bi], jj=jl, hh=hd, cl=col, r=r0,
                                     tp=tpos, ab=at1[gp % 4]:
                                  e.matmul(t[r:r + 64, 0:128],
                                           v_tok[:, jj * H + hh * 64:
                                                 jj * H + hh * 64 + 64],
                                           ab[:, cl:cl + 128],
                                           start=True, stop=False,
                                           tile_position=tp,
                                           skip_group_check=True),
                              waits=([bank_free[bi]] + gd[("atp", gp)] + t_vev
                                     if s == 0 else [])))
                toks.append(E("tensor",
                              lambda e, t=pbf[bi], jj=jl, hh=hd, cl=col, r=r0,
                                     tp=tpos, ab=at2[gp % 4]:
                                  e.matmul(t[r:r + 64, 0:128],
                                           v_tok[0:6, (jj + 1) * H + hh * 64:
                                                 (jj + 1) * H + hh * 64 + 64],
                                           ab[0:6, cl:cl + 128],
                                           start=False, stop=True,
                                           tile_position=tp,
                                           skip_group_check=True)))
            bank_done[bi] = toks[-1]
            gd["avmp"][gp] = [toks[-1]]
            gd[("avmm", gp)] = toks
            bank_free[bi] = E("scalar",
                              lambda e, t=pbf[bi], c=hp, jj=jl:
                                  e.activation(o_fm[c][:, jj * 128:
                                                       jj * 128 + 128],
                                               t[:, 0:128], AF.Identity),
                              waits=[bank_done[bi]] + list(gd["ofm"]))
            gd[("oev", gp)] = bank_free[bi]

        def xres_prefetch(jl):
            gqb = 4 * a + jl
            par = jl % 2
            xw = list(gd["xres"][par])
            if gd["first_xres"]:
                xw += list(t_ln_all)
                gd["first_xres"] = False
            gd[("xr", jl)] = E("sync",
                               lambda e, g=gqb, d=x_res[par]:
                                   e.dma_start(d[:],
                                               x_tok_h[3 + g * 128:
                                                       3 + g * 128 + 128, :]),
                               waits=xw, stream=f"dR{par}")

        def oproj(jl):
            gqb = 4 * a + jl
            par = jl % 2
            oev = [gd[("oev", 32 * a + 8 * jl + k)] for k in range(8)]
            t_xr = gd[("xr", jl)]
            accts = []
            for half in range(2):
                bi = 0
                for ic in range(8):
                    E("tensor",
                      lambda e, i=ic, t=pbf[bi], jj=jl, hh=half:
                          e.matmul(t[:], o_fm[i][:, jj * 128:jj * 128 + 128],
                                   wo_sb[i][:, hh * 512:(hh + 1) * 512],
                                   start=(i == 0), stop=False,
                                   skip_group_check=True),
                      waits=([bank_free[bi]] + oev + t_beff + [t_cAll]
                             if ic == 0 else []))
                tk = E("tensor",
                       lambda e, t=pbf[bi], hh=half:
                           e.matmul(t[:], ones_sb[0:1, :],
                                    beff_sb[0:1, hh * 512:(hh + 1) * 512],
                                    start=False, stop=True,
                                    skip_group_check=True),
                       waits=[])
                bank_done[bi] = tk
                tv = E("vector",
                       lambda e, t=pbf[bi], g=gqb, hh=half, xr=x_res[par]:
                           e.tensor_tensor(acc[:, g * H + hh * 512:
                                               g * H + hh * 512 + 512],
                                           t[:], xr[:, hh * 512:hh * 512 + 512],
                                           ALU.add),
                       waits=[bank_done[bi], t_xr])
                bank_free[bi] = tv
                accts.append(tv)
            gd["xres"][par] = accts
            gd[("acc", gqb)] = accts

        if upto in ("sm", "tp", "av"):
            sc_pair(0)
            sm_pair(0)
            sc_pair(1)
            sm_pair(1)
            if upto in ("tp", "av"):
                t_tp = tp_group(0)
                t_at = at_evac(0, t_tp)
                if upto == "av":
                    av_pair(0, t_at)
                    av_pair(1, t_at)
            return
        for g in range(18):
            if 1 <= g <= 16:
                gl = g - 1
                t_tp = tp_group(gl)
                at_evac(gl, t_tp)
            if g < 16:
                sc_pair(2 * g)
                sm_pair(2 * g)
                sc_pair(2 * g + 1)
                sm_pair(2 * g + 1)
            if g >= 2:
                ga = g - 2
                if ga % 4 == 0:
                    xres_prefetch(ga // 4)
                av_pair(2 * ga, None)
                av_pair(2 * ga + 1, None)
                if (2 * ga + 1) % 8 == 7:
                    oproj((2 * ga + 1) // 8)

        gd["qkv"] = [p.last("tensor")]
        gd["ofm"] = [p.last("tensor")]

    if stage in ("ln", "ln0"):
        dump_and_finish("xn")
    if stage in ("a1", "a2", "a3", "a4"):
        emit_atile(0, {"a1": "qkv", "a2": "sm", "a3": "tp", "a4": "av"}[stage])
        base = p.barrier_tokens()
        dumps = []
        if stage == "a1":
            for ic in range(8):
                dumps.append((y_h[ic * 128:(ic + 1) * 128, 0:512], q_fm[ic][:]))
                dumps.append((y_h[ic * 128:(ic + 1) * 128, 512:1024],
                              k_fm[ic][:, 0:512]))
            for tb in range(5):
                dumps.append((y_h[1024 + tb * 128:1152 + tb * 128, :],
                              v_tok[:, tb * H:(tb + 1) * H]))
        elif stage == "a2":
            dumps.append((y_h[0:128, 0:268], s_m[0][:]))
            dumps.append((y_h[128:256, 0:268], p_sb[0][:]))
            dumps.append((y_h[256:384, 0:268], p_n[0][:]))
            dumps.append((y_h[384:512, 0:2], nm_sb[0][:]))
            dumps.append((y_h[512:640, 0:2], z_sb[0][:]))
            dumps.append((y_h[640:768, 0:2], rz_sb[0][:]))
            dumps.append((y_h[768:896, 0:268], s_m[1][:]))
            dumps.append((y_h[896:1024, 0:268], p_n[1][:]))
        elif stage == "a3":
            dumps.append((y_h[0:128, 0:512], at1[0][:]))
            dumps.append((y_h[128:134, 0:512], at2[0][0:6, :]))
        elif stage == "a4":
            dumps.append((y_h[0:128, 0:512], o_fm[0][:]))
            dumps.append((y_h[128:256, 0:512], o_fm[1][:]))
        tprev = None
        for dst, srcap in dumps:
            tprev = E("gpsimd", lambda e, d=dst, s=srcap: e.dma_start(d, s),
                      waits=list(base) + ([tprev] if tprev else []),
                      stream="dY0")
        t_last_y[0] = tprev
    if stage in ("full", "attn", "b1", "b2g"):
        for a in range(NA):
            emit_atile(a)
    if stage == "attn":
        dump_and_finish("acc")

    t_phA = p.barrier_tokens()

    g_wg = [[], []]
    g_wu = [[], []]
    hm_prev = [None, None, None, None]
    nhc = (HC if stage in ("full", "b1", "b2g")
           else (1 if stage == "b1g" else 0))
    if nhc:
        E("scalar",
          lambda e: e.add_instruction(mybir.InstLoadActFuncSet(
              name=nc.get_next_instruction_name(),
              act_func_set_id=18, ins=[], outs=[])),
          waits=t_phA)
    for hc in range(nhc):
        par = hc % 2
        t_wg = E("gpsimd",
                 lambda e, c=hc, d=wgst[par]:
                     e.dma_start(d[:], wg_hh[c * 128:(c + 1) * 128, :]),
                 waits=list(g_wg[par]) + (t_phA if hc == 0 else []),
                 stream=f"dG{par}")
        t_wu = E("gpsimd",
                 lambda e, c=hc, d=wust[par]:
                     e.dma_start(d[:], wu_hh[c * 128:(c + 1) * 128, :]),
                 waits=list(g_wu[par]), stream=f"dU{par}")
        gate_last = []
        for ic in range(8):
            for g in range(4):
                bi = g
                tk = E("tensor",
                       lambda e, i=ic, gg=g, t=pbf[bi], w=wgst[par]:
                           e.matmul(t[:], w[:, i * 128:(i + 1) * 128],
                                    xn[i][:, 3 + gg * 512:3 + gg * 512 + 512],
                                    start=(i == 0), stop=(i == 7),
                                    skip_group_check=True),
                       waits=(([bank_free[bi]] if ic == 0 else []) +
                              ([t_wg] + (t_phA if hc == 0 else [])
                               if (ic == 0 and g == 0) else [])))
                if ic == 7:
                    bank_done[bi] = tk
                    gate_last.append(tk)
        g_wg[par] = [gate_last[-1]]
        t_silu = []
        for g in range(4):
            bi = g
            ts_ = E("scalar",
                    lambda e, gg=g, t=pbf[bi]:
                        e.activation(htmp[:, gg * 512:gg * 512 + 512], t[:],
                                     AF.Silu, scale=QS),
                    waits=[bank_done[bi], hm_prev[g]] +
                          (t_phA if hc == 0 and g == 0 else []))
            bank_free[bi] = ts_
            t_silu.append(ts_)
        up_last = []
        for ic in range(8):
            for g in range(4):
                bi = 4 + g
                tk = E("tensor",
                       lambda e, i=ic, gg=g, t=pbf[bi], w=wust[par]:
                           e.matmul(t[:], w[:, i * 128:(i + 1) * 128],
                                    xn[i][:, 3 + gg * 512:3 + gg * 512 + 512],
                                    start=(i == 0), stop=(i == 7),
                                    skip_group_check=True),
                       waits=(([bank_free[bi]] if ic == 0 else []) +
                              ([t_wu] if (ic == 0 and g == 0) else [])))
                if ic == 7:
                    bank_done[bi] = tk
                    up_last.append(tk)
        g_wu[par] = [up_last[-1]]
        for g in range(4):
            bi = 4 + g
            bank_free[bi] = E("vector",
                              lambda e, c=hc, gg=g, t=pbf[bi]:
                                  e.tensor_tensor(
                                      h_sb[c][:, gg * 512:gg * 512 + 512],
                                      htmp[:, gg * 512:gg * 512 + 512],
                                      t[:], ALU.mult),
                              waits=[bank_done[bi], t_silu[g]] +
                                    (t_phA if hc == 0 and g == 0 else []))
            hm_prev[g] = bank_free[bi]

    if stage == "b1":
        dump_and_finish("h")
    if stage == "b1g":
        base = p.barrier_tokens()
        dumps = [(0, wgst[0][:, 0:1024]),
                 (1, htmp[:, 0:1024]),
                 (2, h_sb[0][:, 0:1024]),
                 (3, xn[0][:, 0:1024]),
                 (4, pbf[0][:, 0:512]),
                 (5, pbf[4][:, 0:512])]
        tprev = None
        for row, srcap in dumps:
            tv = E("vector",
                   lambda e, s=srcap, n=srcap.shape[-1]:
                       e.tensor_copy(y_sb[0][:, 0:n], s),
                   waits=list(base) + ([tprev] if tprev else []))
            tprev = E("sync",
                      lambda e, rr=row: e.dma_start(
                          y_h[rr * 128:(rr + 1) * 128, :], y_sb[0][:]),
                      waits=[tv], stream="dY0")
        t_last_y[0] = tprev

    t_phB = p.barrier_tokens()

    t_wd = []
    for hc in range(HC if stage in ("full", "b2g") else 0):
        t_wd.append(E("gpsimd",
                      lambda e, c=hc:
                          e.dma_start(wdT[c][:],
                                      wd_hh[c * 128:(c + 1) * 128, :]),
                      waits=(t_phB if hc == 0 else []),
                      stream="dD"))

    t_ystore = [None, None]
    for tb in range(NQB if stage == "full" else (1 if stage == "b2g" else 0)):
        evs = []
        for half in range(2):
            bi = (tb % 2) * 2 + half
            tk = None
            for hc in range(HC):
                w = []
                if hc == 0:
                    w = [bank_free[bi], t_wd[-1]] + (t_phB
                                                     if tb == 0 and half == 0
                                                     else [])
                tk = E("tensor",
                       lambda e, c=hc, tt=tb, hh=half, t=pbf[bi]:
                           e.matmul(t[:], h_sb[c][:, tt * 128:tt * 128 + 128],
                                    wdT[c][:, hh * 512:(hh + 1) * 512],
                                    start=(c == 0), stop=(c == HC - 1),
                                    skip_group_check=True),
                       waits=w)
            bank_done[bi] = tk
            tv = E("vector",
                   lambda e, tt=tb, hh=half, t=pbf[bi], d=y_sb[tb % 2]:
                       e.scalar_tensor_tensor(
                           d[:, hh * 512:hh * 512 + 512], t[:], QS2,
                           acc[:, tt * H + hh * 512:tt * H + hh * 512 + 512],
                           ALU.mult, ALU.add),
                   waits=[bank_done[bi]] +
                         ([t_ystore[tb % 2]] if t_ystore[tb % 2] else []))
            bank_free[bi] = tv
            evs.append(tv)
        t_ystore[tb % 2] = E("sync",
                             lambda e, tt=tb, s=y_sb[tb % 2]:
                                 e.dma_start(y_h[tt * 128:tt * 128 + 128, :],
                                             s[:]),
                             waits=evs, stream=f"dY{tb % 2}")
        if tb == NQB - 1 or stage == "b2g":
            t_last_y[0] = t_ystore[tb % 2]
    if stage == "b2g":
        base = p.barrier_tokens()
        tprev = None
        for row, srcap in ((1, wdT[0][:, 0:1024]), (2, wdT[23][:, 0:1024]),
                           (3, acc[:, 0:1024]), (4, h_sb[0][:, 0:1024])):
            tv = E("vector",
                   lambda e, s=srcap: e.tensor_copy(y_sb[1][:], s),
                   waits=list(base) + ([tprev] if tprev else []))
            tprev = E("sync",
                      lambda e, rr=row: e.dma_start(
                          y_h[rr * 128:(rr + 1) * 128, :], y_sb[1][:]),
                      waits=[tv], stream="dY1")
        t_last_y[0] = tprev

    final_tok = t_last_y[0]
    all_keys = set(p.nkey)
    dma_keys = all_keys - set(("tensor", "vector", "scalar"))
    waited = {k: set() for k in all_keys}
    for k in all_keys:
        if k not in dma_keys:
            waited[k] = set(range(p.nkey[k]))
    for e in ENGINES:
        for _, ws, _k in p.ops[e]:
            for (wk_, wi) in ws:
                waited[wk_].add(wi)
    waited[final_tok[0]].add(final_tok[1])
    rank = {}
    for k in all_keys:
        if k in dma_keys:
            rank[k] = {i: i + 1 for i in range(p.nkey[k])}
        else:
            rank[k] = {i: r + 1 for r, i in enumerate(sorted(waited[k]))}
    unit = {k: (16 if k in dma_keys else 1) for k in all_keys}

    import contextlib
    with contextlib.ExitStack() as stk:
        semh = {k: stk.enter_context(nc.semaphore("c_" + str(k)))
                for k in all_keys}
        kcnt = {k: 0 for k in all_keys}
        with nc.Block() as block:
            def make(eng_name):
                def run(e):
                    if eng_name == "scalar":
                        e.add_instruction(mybir.InstLoadActFuncSet(
                            name=nc.get_next_instruction_name(),
                            act_func_set_id=6, ins=[], outs=[]))
                    for fn, ws, key in p.ops[eng_name]:
                        for (wk_, wi) in ws:
                            e.wait_ge(semh[wk_], rank[wk_][wi] * unit[wk_])
                        inst = fn(e)
                        i = kcnt[key]
                        kcnt[key] += 1
                        if key in dma_keys or i in waited[key]:
                            inst.then_inc(semh[key], unit[key])
                    if eng_name == "sync":
                        e.wait_ge(semh[final_tok[0]],
                                  rank[final_tok[0]][final_tok[1]] *
                                  unit[final_tok[0]])
                return run

            block.sync(make("sync"))
            block.gpsimd(make("gpsimd"))
            block.tensor(make("tensor"))
            block.vector(make("vector"))
            block.scalar(make("scalar"))
    return nc


_NC = None


def _get_nc():
    global _NC
    if _NC is None:
        _NC = build_nc()
    return _NC


def _band_mask(first, last):
    m = np.full((128, 134), NEG, np.float32)
    for r in range(128):
        lo = max(r, 3) if first else r
        hi = min(r + 6, 130) if last else r + 6
        m[r, lo:hi + 1] = 0.0
    return np.concatenate([m, m], axis=1).astype(ml_dtypes.bfloat16)


def kernel(L0_attn_hidden_states, L0_attn_w_q, L0_attn_b_q, L0_attn_w_k,
           L0_attn_b_k, L0_attn_w_v, L0_attn_b_v, L0_attn_w_o, L0_attn_b_o,
           L0_ffn_w_gate, L0_ffn_w_up, L0_ffn_w_down):
    x = np.ascontiguousarray(np.asarray(L0_attn_hidden_states, np.float32))
    wq = np.ascontiguousarray(np.asarray(L0_attn_w_q, np.float32))
    wk = np.ascontiguousarray(np.asarray(L0_attn_w_k, np.float32))
    wv = np.ascontiguousarray(np.asarray(L0_attn_w_v, np.float32))
    wo = np.ascontiguousarray(np.asarray(L0_attn_w_o, np.float32))
    bq = np.asarray(L0_attn_b_q, np.float32)
    bk = np.asarray(L0_attn_b_k, np.float32)
    bv = np.asarray(L0_attn_b_v, np.float32)
    bo = np.asarray(L0_attn_b_o, np.float32)
    wg = np.asarray(L0_ffn_w_gate, np.int32)
    wu = np.asarray(L0_ffn_w_up, np.int32)
    wd = np.asarray(L0_ffn_w_down, np.int32)

    wg_b = np.ascontiguousarray(
        wg.reshape(HC, 128, 8, 128).transpose(0, 3, 2, 1).reshape(FH, H))
    wu_b = np.ascontiguousarray(
        wu.reshape(HC, 128, 8, 128).transpose(0, 3, 2, 1).reshape(FH, H))
    wd_b = np.ascontiguousarray(wd.T)

    shared = dict(
        wq=wq, wk=wk, wv=wv, wo=wo,
        bq=np.ascontiguousarray(bq.reshape(8, 128).T),
        bk=np.ascontiguousarray(bk.reshape(8, 128).T),
        bv=np.ascontiguousarray(bv.reshape(8, 128).T),
        bo_row=np.ascontiguousarray(bo.reshape(1, H)),
        wg_b=wg_b, wu_b=wu_b, wd_b=wd_b,
        ident=np.eye(128, dtype=ml_dtypes.bfloat16),
        ident_f=np.eye(128, dtype=np.float32),
        ones_row=np.ones((1, 128), dtype=ml_dtypes.bfloat16),
        eps_col=np.full((128, 1), EPS, np.float32),
    )
    m_mid = _band_mask(False, False)
    m_first = _band_mask(True, False)
    m_last = _band_mask(False, True)

    in_maps = []
    for core in range(NCORES):
        b_i, qi = core // 4, core % 4
        t0 = qi * TOK
        x_ext = np.zeros((EXT, H), np.float32)
        lo = t0 - 3
        s0, s1 = max(0, lo), min(S, lo + EXT)
        x_ext[s0 - lo:s1 - lo] = x[b_i, s0:s1]
        im = dict(shared)
        im["x_tok"] = x_ext
        im["mask_first"] = m_first if qi == 0 else m_mid
        im["mask_mid"] = m_mid
        im["mask_last"] = m_last if qi == 3 else m_mid
        in_maps.append(im)

    res = run_bass_kernel_spmd(_get_nc(), in_maps, core_ids=list(range(NCORES)))
    y = np.empty((B, S, H), np.float32)
    for core in range(NCORES):
        b_i, qi = core // 4, core % 4
        y[b_i, qi * TOK:(qi + 1) * TOK] = res.results[core]["y"]
    return y


if __name__ == "__main__":
    np.random.seed(0)
    import reference as ref
    ins = {k: np.asarray(v) for k, v in ref.setup_inputs().items()}
    out = kernel(**ins)
    exp = np.asarray(ref.reference(**ins))
    print("rvr:", float(np.mean((out - exp) ** 2) / np.mean(exp ** 2)))
    print("maxabs:", float(np.abs(out - exp).max()))
